# version 11 (final)
# speedup vs baseline: 1.2481x; 1.0018x over previous
import numpy as np
import concourse.bass as bass
import concourse.mybir as mybir
from concourse.alu_op_type import AluOpType as ALU
from contextlib import ExitStack

F32 = mybir.dt.float32
BF16 = mybir.dt.bfloat16
U32 = mybir.dt.uint32
U8 = mybir.dt.uint8
AF = mybir.ActivationFunctionType
AX = mybir.AxisListType

BL = 2
T = 2048
D = 1024
NCH = 8


class Sem:
    def __init__(self, nc, name):
        self.h = nc.semaphore(name).__enter__()
        self.n = 0

    def bump(self, k):
        self.n += k
        return self.n


class Ker:
    def __init__(self, nc):
        self.nc = nc
        self.ctx = ExitStack()
        self._scopes = [self.ctx]
        self.sems = {}

    def push_scope(self):
        self._scopes.append(ExitStack())

    def pop_scope(self):
        self._scopes.pop().close()

    def sem(self, name):
        if name not in self.sems:
            self.sems[name] = Sem(self.nc, name)
        return self.sems[name]

    def sb(self, name, shape, dt=F32):
        return self._scopes[-1].enter_context(self.nc.sbuf_tensor(name, shape, dt))

    def ps(self, name, shape, dt=F32):
        return self._scopes[-1].enter_context(self.nc.psum_tensor(name, shape, dt))


def fin(sem, inst):
    inst.then_inc(sem.h, 1)
    return sem.bump(1)


def fdma(sem, inst):
    inst.then_inc(sem.h, 16)
    return sem.bump(16)


def emit_l0(k: Ker, x_fm, x_tm, wkT, wvT, wrT, woT, prm, s1_tm, scr_r, scr_ry):
    nc = k.nc
    mixk = k.sb("l0_mixk", [128, NCH, T], BF16)
    mixv = k.sb("l0_mixv", [128, NCH, T], BF16)
    xbuf = k.sb("l0_xbuf", [128, T + 1], F32)
    wsb = k.sb("l0_wsb", [128, NCH, D], BF16)
    wsb2 = k.sb("l0_wsb2", [128, NCH, D], BF16)
    kj = k.sb("l0_kj", [128, T], F32)
    bufA = k.sb("l0_A", [128, T], F32)
    bufB = k.sb("l0_B", [128, T], F32)
    bufV = k.sb("l0_V", [128, T], F32)
    bufC = k.sb("l0_C", [128, T + 1], F32)
    bufD = k.sb("l0_D", [128, T + 1], F32)
    bufE = k.sb("l0_E", [128, T + 1], F32)
    ewb = k.sb("l0_ewb", [128, T], F32)
    wbt = k.sb("l0_wbt", [128, T], F32)
    rj = k.sb("l0_rj", [128, T], BF16)
    rj2 = k.sb("l0_rj2", [128, T], BF16)
    ryst = k.sb("l0_ryst", [128, T], BF16)
    ryst2 = k.sb("l0_ryst2", [128, T], BF16)
    prms = k.sb("l0_prm", [128, 8, 5], F32)
    dcol = k.sb("l0_dcol", [128, 8, 4], F32)
    rybuf = k.sb("l0_rybuf", [128, NCH, 128], BF16)
    resid = k.sb("l0_resid", [128, D], F32)
    stile = k.sb("l0_stile", [128, D], F32)
    junk = k.sb("l0_junk", [128, D], F32)
    ocol = k.sb("l0_ocol", [128, 8], F32)
    psK = k.ps("l0_psK", [128, 4, 512], F32)
    psV = k.ps("l0_psV", [128, 4, 512], F32)
    psO = psK

    ld = k.sem("ld"); st = k.sem("st"); tc = k.sem("tc")
    vc = k.sem("vc"); sc = k.sem("sc")

    l0 = ld.n
    with nc.Block(no_gpsimd_drain=True) as b:
        @b.sync
        def _(sy):
            fdma(ld, sy.dma_start(out=prms[:, :, :], in_=prm))
            sy.wait_ge(ld.h, l0 + 16)
        @b.vector
        def _(v):
            v.wait_ge(ld.h, l0 + 16)
            v.memset(ocol[:, 0:1], 1e-5)
            v.memset(xbuf[:, 0:1], 0.0)
            v.memset(ewb[:, :], 0.0)
            v.memset(wbt[:, :], 0.0)
    with nc.Block(no_gpsimd_drain=True) as b:
        @b.scalar
        def _(s):
            t1 = fin(sc, s.activation(junk[:, 0:8], prms[:, :, 3], AF.Exp))
            s.wait_ge(sc.h, t1)
            s.activation(dcol[:, :, 2], prms[:, :, 4], AF.Exp)
        @b.vector
        def _(v):
            v.tensor_copy(dcol[:, :, 3], prms[:, :, 4])
    with nc.Block(no_gpsimd_drain=True) as b:
        @b.vector
        def _(v):
            t1 = fin(vc, v.tensor_scalar(dcol[:, :, 0], junk[:, 0:8], -1.0, None,
                                         ALU.mult))
            v.wait_ge(vc.h, t1)
    with nc.Block(no_gpsimd_drain=True) as b:
        @b.scalar
        def _(s):
            s.activation(dcol[:, :, 1], dcol[:, :, 0], AF.Exp)

    for bi in range(BL):
        _mix_phase(k, bi, x_fm, xbuf, mixk, prms, 2, bufA, ld, vc)
        _load_weight(k, wrT, wsb, ld)
        _proj_spill_r(k, bi, mixk, wsb, psK, rj, scr_r, ld, tc, sc, st)

        _mix_phase(k, bi, x_fm, xbuf, mixk, prms, 0, bufA, ld, vc)
        _mix_phase(k, bi, x_fm, xbuf, mixv, prms, 1, bufA, ld, vc)
        _load_weight(k, wkT, wsb, ld)
        _load_weight(k, wvT, wsb2, ld)

        T0 = tc.n; C0 = sc.n; V0 = vc.n
        rjp = [k.sem("rjp0"), k.sem("rjp1")]
        R0 = [rjp[0].n, rjp[1].n]
        sty = [k.sem("sty0"), k.sem("sty1")]
        S0 = [sty[0].n, sty[1].n]
        with nc.Block(no_gpsimd_drain=True) as b:
            @b.sync
            def _(sy):
                for j in range(NCH + 2):
                    if j < NCH:
                        if j >= 2:
                            sy.wait_ge(vc.h, V0 + 12 * (j - 1))
                        fdma(rjp[j % 2], sy.dma_start(
                            out=(rj if j % 2 == 0 else rj2)[:, :],
                            in_=scr_r[bi, j]))
                    if j >= 2:
                        jj = j - 2
                        sy.wait_ge(vc.h, V0 + 12 * jj + 12)
                        fdma(sty[jj % 2], sy.dma_start(
                            out=scr_ry[bi, jj],
                            in_=(ryst if jj % 2 == 0 else ryst2)[:, :]))
                sy.wait_ge(sty[0].h, S0[0] + 64)
                sy.wait_ge(sty[1].h, S0[1] + 64)

            @b.tensor
            def _(t):
                for j in range(NCH):
                    if j > 0:
                        t.wait_ge(sc.h, C0 + 7 * (j - 1) + 1)
                        t.wait_ge(vc.h, V0 + 12 * (j - 1) + 1)
                    for tp in range(4):
                        for ci in range(NCH):
                            mm = t.matmul(psK[:, tp, :],
                                          wsb[:, ci, j * 128:(j + 1) * 128],
                                          mixk[:, ci, tp * 512:(tp + 1) * 512],
                                          start=(ci == 0), stop=(ci == NCH - 1))
                        if tp == 3:
                            fin(tc, mm)
                    for tp in range(4):
                        for ci in range(NCH):
                            mm = t.matmul(psV[:, tp, :],
                                          wsb2[:, ci, j * 128:(j + 1) * 128],
                                          mixv[:, ci, tp * 512:(tp + 1) * 512],
                                          start=(ci == 0), stop=(ci == NCH - 1))
                        if tp == 3:
                            fin(tc, mm)

            @b.scalar
            def _(s):
                for j in range(NCH):
                    s.wait_ge(tc.h, T0 + 2 * j + 1)
                    if j > 0:
                        s.wait_ge(vc.h, V0 + 12 * (j - 1) + 11)
                    c1 = fin(sc, s.copy(kj[:, :],
                                        psK.ap().rearrange("p a b -> p (a b)")))
                    s.wait_ge(sc.h, c1)
                    fin(sc, s.activation(ewb[:, :], kj[:, :], AF.Identity,
                                         bias=dcol[:, j, 1:2], scale=0.0))
                    fin(sc, s.activation(wbt[:, :], kj[:, :], AF.Identity,
                                         bias=dcol[:, j, 0:1], scale=0.0))
                    fin(sc, s.activation(bufA[:, :], kj[:, :], AF.Exp))
                    s.wait_ge(vc.h, V0 + 12 * j + 9)
                    c5 = fin(sc, s.activation(bufA[:, :], bufA[:, :], AF.Exp))
                    s.wait_ge(sc.h, c5)
                    s.wait_ge(vc.h, V0 + 12 * j + 10)
                    c6 = fin(sc, s.activation(ewb[:, :], bufD[:, 1:T + 1], AF.Ln))
                    s.wait_ge(sc.h, c6)
                    fin(sc, s.activation(wbt[:, :], ewb[:, :], AF.Exp, scale=-1.0))

            @b.vector
            def _(v):
                for j in range(NCH):
                    rjb = rj if j % 2 == 0 else rj2
                    ryb = ryst if j % 2 == 0 else ryst2
                    v.wait_ge(tc.h, T0 + 2 * j + 2)
                    t1 = fin(vc, v.tensor_copy(bufV[:, :],
                                          psV.ap().rearrange("p a b -> p (a b)")))
                    fin(vc, v.memset(bufC[:, 0:1], 0.0))
                    v.memset(bufD[:, 0:1], 0.0)
                    v.memset(bufE[:, 0:1], -1e30)
                    v.wait_ge(vc.h, t1)
                    v.wait_ge(sc.h, C0 + 7 * j + 4)
                    t2 = fin(vc, v.tensor_tensor(bufB[:, :], bufA[:, :], bufV[:, :],
                                                 ALU.mult))
                    v.wait_ge(vc.h, t2)
                    fin(vc, v.tensor_tensor_scan(bufC[:, 1:T + 1], ewb[:, :],
                                                 bufB[:, :], 0.0, ALU.mult,
                                                 ALU.add))
                    fin(vc, v.tensor_tensor_scan(bufD[:, 1:T + 1], ewb[:, :],
                                                 bufA[:, :], 0.0, ALU.mult,
                                                 ALU.add))
                    t4 = fin(vc, v.tensor_tensor_scan(bufE[:, 1:T + 1], wbt[:, :],
                                                      kj[:, :], -1e30, ALU.add,
                                                      ALU.max))
                    v.wait_ge(vc.h, t4)
                    t5 = fin(vc, v.scalar_tensor_tensor(bufV[:, :], bufC[:, 0:T],
                                                        dcol[:, j, 2:3], bufB[:, :],
                                                        ALU.mult, ALU.add))
                    v.wait_ge(vc.h, t5)
                    t6 = fin(vc, v.scalar_tensor_tensor(bufB[:, :], bufD[:, 0:T],
                                                        dcol[:, j, 2:3], bufA[:, :],
                                                        ALU.mult, ALU.add))
                    v.wait_ge(vc.h, t6)
                    t7 = fin(vc, v.scalar_tensor_tensor(bufA[:, :], bufE[:, 0:T],
                                                        dcol[:, j, 3:4], kj[:, :],
                                                        ALU.add, ALU.max))
                    v.wait_ge(sc.h, C0 + 7 * j + 5)
                    t8 = fin(vc, v.scalar_tensor_tensor(bufD[:, 1:T + 1], bufA[:, :],
                                                        1e-6, bufB[:, :],
                                                        ALU.mult, ALU.add))
                    v.wait_ge(sc.h, C0 + 7 * j + 7)
                    t9 = fin(vc, v.tensor_tensor(bufB[:, :], bufV[:, :], wbt[:, :],
                                                 ALU.mult))
                    v.wait_ge(vc.h, t9)
                    v.wait_ge(rjp[j % 2].h, R0[j % 2] + 16 * (j // 2 + 1))
                    if j >= 2:
                        v.wait_ge(sty[j % 2].h, S0[j % 2] + 16 * (j // 2))
                    fin(vc, v.tensor_tensor(ryb[:, :], rjb[:, :], bufB[:, :],
                                            ALU.mult))

        _load_weight(k, woT, wsb, ld)
        for tci in range(16):
            l0 = ld.n; t0 = tc.n
            with nc.Block(no_gpsimd_drain=True) as b:
                @b.sync
                def _(sy):
                    fdma(ld, sy.dma_start(
                        out=rybuf[:, :, :],
                        in_=scr_ry[bi, :, :, tci * 128:(tci + 1) * 128]
                            .rearrange("j p t -> p j t")))
                    fdma(ld, sy.dma_start(out=resid[:, :],
                                          in_=x_tm[bi, tci * 128:(tci + 1) * 128, :]))
                @b.tensor
                def _(t):
                    t.wait_ge(ld.h, l0 + 32)
                    for half in range(2):
                        for ci in range(NCH):
                            mm = t.matmul(psO[:, half, :],
                                          rybuf[:, ci, :],
                                          wsb[:, ci, half * 512:(half + 1) * 512],
                                          start=(ci == 0), stop=(ci == NCH - 1))
                    fin(tc, mm)
                @b.vector
                def _(v):
                    v.wait_ge(tc.h, t0 + 1)
                    v.wait_ge(ld.h, l0 + 32)
            _ln_chunk(k, psO[:, 0:2, :].rearrange("p a b -> p (a b)"), resid, stile,
                      junk, ocol, vc, sc,
                      store=(s1_tm[bi, tci * 128:(tci + 1) * 128, :], st))


def _load_weight(k, wdram, wtile, ld):
    nc = k.nc
    l0 = ld.n
    with nc.Block(no_gpsimd_drain=True) as b:
        @b.gpsimd
        def _(g):
            fdma(ld, g.dma_start(out=wtile[:, :, :],
                                 in_=wdram.rearrange("(ci p) o -> p ci o", p=128)))
            g.wait_ge(ld.h, l0 + 16)


def _mix_phase(k, bi, x_fm, xbuf, mixbuf, prms, mix_idx, dscr, ld, vc):
    nc = k.nc
    for i in range(NCH):
        l0 = ld.n; v0 = vc.n
        with nc.Block(no_gpsimd_drain=True) as b:
            @b.sync
            def _(sy):
                fdma(ld, sy.dma_start(out=xbuf[:, 1:T + 1],
                                      in_=x_fm[bi, i * 128:(i + 1) * 128, :]))
            @b.vector
            def _(v):
                v.wait_ge(ld.h, l0 + 16)
                t1 = fin(vc, v.tensor_tensor(dscr[:, :], xbuf[:, 1:T + 1],
                                             xbuf[:, 0:T], ALU.subtract))
                v.wait_ge(vc.h, t1)
                v.scalar_tensor_tensor(mixbuf[:, i, :], dscr[:, :],
                                       prms[:, i, mix_idx:mix_idx + 1],
                                       xbuf[:, 0:T], ALU.mult, ALU.add)


def _proj_spill_r(k, bi, mixr, wsb, psK, rj, scr_r, ld, tc, sc, st):
    nc = k.nc
    for j in range(NCH):
        t0 = tc.n; c0 = sc.n; s0 = st.n
        with nc.Block(no_gpsimd_drain=True) as b:
            @b.tensor
            def _(t):
                for tp in range(4):
                    for ci in range(NCH):
                        mm = t.matmul(psK[:, tp, :],
                                      wsb[:, ci, j * 128:(j + 1) * 128],
                                      mixr[:, ci, tp * 512:(tp + 1) * 512],
                                      start=(ci == 0), stop=(ci == NCH - 1))
                fin(tc, mm)
            @b.scalar
            def _(s):
                s.wait_ge(tc.h, t0 + 1)
                t1 = fin(sc, s.activation(rj[:, :],
                                          psK.ap().rearrange("p a b -> p (a b)"),
                                          AF.Sigmoid))
                s.wait_ge(sc.h, t1)
                fdma(st, s.dma_start(out=scr_r[bi, j], in_=rj[:, :]))
                s.wait_ge(st.h, s0 + 16)


def _wkv_chunk(k, bi, j, kj, A, B, V, C, Dd, E, ewb, wbt, dcol, rj, ryst, vc, sc):
    nc = k.nc
    c0 = sc.n; v0 = vc.n
    with nc.Block(no_gpsimd_drain=True) as b:
        @b.scalar
        def _(s):
            t1 = fin(sc, s.activation(A[:, :], kj[:, :], AF.Exp))
            s.activation(ewb[:, :], kj[:, :], AF.Identity, bias=dcol[:, j, 1:2],
                         scale=0.0)
            t2 = fin(sc, s.activation(wbt[:, :], kj[:, :], AF.Identity,
                                      bias=dcol[:, j, 0:1], scale=0.0))
        @b.vector
        def _(v):
            v.memset(C[:, 0:1], 0.0)
            v.memset(Dd[:, 0:1], 0.0)
            v.memset(E[:, 0:1], -1e30)
            v.wait_ge(sc.h, c0 + 2)
            t1 = fin(vc, v.tensor_tensor(B[:, :], A[:, :], V[:, :], ALU.mult))
            v.wait_ge(vc.h, t1)
            t2 = fin(vc, v.tensor_tensor_scan(C[:, 1:T + 1], ewb[:, :], B[:, :],
                                              0.0, ALU.mult, ALU.add))
            t3 = fin(vc, v.tensor_tensor_scan(Dd[:, 1:T + 1], ewb[:, :], A[:, :],
                                              0.0, ALU.mult, ALU.add))
            t4 = fin(vc, v.tensor_tensor_scan(E[:, 1:T + 1], wbt[:, :], kj[:, :],
                                              -1e30, ALU.add, ALU.max))
            v.wait_ge(vc.h, t4)
            t5 = fin(vc, v.scalar_tensor_tensor(V[:, :], C[:, 0:T], dcol[:, j, 2:3],
                                                B[:, :], ALU.mult, ALU.add))
            v.wait_ge(vc.h, t5)
            t6 = fin(vc, v.scalar_tensor_tensor(B[:, :], Dd[:, 0:T], dcol[:, j, 2:3],
                                                A[:, :], ALU.mult, ALU.add))
            v.wait_ge(vc.h, t6)
            t7 = fin(vc, v.scalar_tensor_tensor(A[:, :], E[:, 0:T], dcol[:, j, 3:4],
                                                kj[:, :], ALU.add, ALU.max))
            v.wait_ge(vc.h, t7)
    c0 = sc.n; v0 = vc.n
    with nc.Block(no_gpsimd_drain=True) as b:
        @b.scalar
        def _(s):
            t1 = fin(sc, s.activation(A[:, :], A[:, :], AF.Exp))
            s.wait_ge(sc.h, t1)
            s.wait_ge(vc.h, v0 + 1)
            t2 = fin(sc, s.activation(ewb[:, :], Dd[:, 1:T + 1], AF.Ln))
            s.wait_ge(sc.h, t2)
            t3 = fin(sc, s.activation(wbt[:, :], ewb[:, :], AF.Exp, scale=-1.0))
        @b.vector
        def _(v):
            v.wait_ge(sc.h, c0 + 1)
            t1 = fin(vc, v.scalar_tensor_tensor(Dd[:, 1:T + 1], A[:, :], 1e-6,
                                                B[:, :], ALU.mult, ALU.add))
            v.wait_ge(sc.h, c0 + 3)
            t2 = fin(vc, v.tensor_tensor(B[:, :], V[:, :], wbt[:, :], ALU.mult))
            v.wait_ge(vc.h, t2)
            v.tensor_tensor(ryst[:, :], rj[:, :], B[:, :], ALU.mult)


def _ln_chunk(k, psum_ap, resid, stile, junk, ocol, vc, sc, gain=None, bias=None,
              store=None):
    nc = k.nc
    v0 = vc.n; c0 = sc.n
    vfin = v0 + (6 if gain is not None else 4)
    with nc.Block(no_gpsimd_drain=True) as b:
        if store is not None:
            @b.sync
            def _(sy):
                sy.wait_ge(vc.h, vfin)
                s0_ = store[1].n
                fdma(store[1], sy.dma_start(out=store[0], in_=stile[:, :]))
                sy.wait_ge(store[1].h, s0_ + 16)
        @b.vector
        def _(v):
            t1 = fin(vc, v.tensor_tensor(stile[:, :], psum_ap, resid[:, :], ALU.add))
            v.wait_ge(vc.h, t1)
            t2 = fin(vc, v.tensor_scalar(junk[:, :], stile[:, :], 1.0 / D, 0.0,
                                         ALU.mult, ALU.add, accum_out=ocol[:, 1:2]))
            v.wait_ge(vc.h, t2)
            t3 = fin(vc, v.tensor_scalar(stile[:, :], stile[:, :], ocol[:, 1:2],
                                         None, ALU.subtract))
            v.wait_ge(vc.h, t3)
            v.wait_ge(sc.h, c0 + 3)
            t4 = fin(vc, v.tensor_scalar(stile[:, :], stile[:, :], ocol[:, 4:5],
                                         None, ALU.mult))
            v.wait_ge(vc.h, t4)
            if gain is not None:
                t5 = fin(vc, v.tensor_tensor(stile[:, :], stile[:, :], gain[:, :],
                                             ALU.mult))
                v.wait_ge(vc.h, t5)
                t6 = fin(vc, v.tensor_tensor(stile[:, :], stile[:, :], bias[:, :],
                                             ALU.add))
                v.wait_ge(vc.h, t6)
        @b.scalar
        def _(s):
            s.wait_ge(vc.h, v0 + 3)
            t1 = fin(sc, s.activation(junk[:, :], stile[:, :], AF.Square,
                                      accum_out=ocol[:, 2:3]))
            s.wait_ge(sc.h, t1)
            t2 = fin(sc, s.activation(ocol[:, 3:4], ocol[:, 2:3], AF.Ln,
                                      bias=ocol[:, 0:1], scale=1.0 / D))
            s.wait_ge(sc.h, t2)
            t3 = fin(sc, s.activation(ocol[:, 4:5], ocol[:, 3:4], AF.Exp, scale=-0.5))


def l0_host_inputs(core, inputs):
    x = np.asarray(inputs["L0_attn_x"])
    xs = x[core * BL:(core + 1) * BL]
    prm = np.stack([
        np.asarray(inputs["L0_attn_time_mix_k"]).reshape(D),
        np.asarray(inputs["L0_attn_time_mix_v"]).reshape(D),
        np.asarray(inputs["L0_attn_time_mix_r"]).reshape(D),
        np.asarray(inputs["L0_attn_time_decay"]).reshape(D),
        np.asarray(inputs["L0_attn_time_first"]).reshape(D),
    ], axis=-1)
    prm = np.ascontiguousarray(prm.reshape(NCH, 128, 5).transpose(1, 0, 2))
    return {
        "x_fm": np.ascontiguousarray(xs.transpose(0, 2, 1)),
        "x_tm": np.ascontiguousarray(xs),
        "wkT": np.ascontiguousarray(np.asarray(inputs["L0_attn_Wk"]).T),
        "wvT": np.ascontiguousarray(np.asarray(inputs["L0_attn_Wv"]).T),
        "wrT": np.ascontiguousarray(np.asarray(inputs["L0_attn_Wr"]).T),
        "woT": np.ascontiguousarray(np.asarray(inputs["L0_attn_Wo"]).T),
        "prm": prm,
    }


def build_l0_program():
    nc = bass.Bass("TRN2", target_bir_lowering=False)
    k = Ker(nc)
    x_fm = nc.dram_tensor("x_fm", [BL, D, T], F32, kind="ExternalInput").ap()
    x_tm = nc.dram_tensor("x_tm", [BL, T, D], F32, kind="ExternalInput").ap()
    wkT = nc.dram_tensor("wkT", [D, D], F32, kind="ExternalInput").ap()
    wvT = nc.dram_tensor("wvT", [D, D], F32, kind="ExternalInput").ap()
    wrT = nc.dram_tensor("wrT", [D, D], F32, kind="ExternalInput").ap()
    woT = nc.dram_tensor("woT", [D, D], F32, kind="ExternalInput").ap()
    prm = nc.dram_tensor("prm", [128, NCH, 5], F32, kind="ExternalInput").ap()
    s1 = nc.dram_tensor("s1", [BL, T, D], F32, kind="ExternalOutput").ap()
    scr_r = nc.dram_tensor("scr_r", [BL, NCH, 128, T], BF16).ap()
    scr_ry = nc.dram_tensor("scr_ry", [BL, NCH, 128, T], BF16).ap()
    emit_l0(k, x_fm, x_tm, wkT, wvT, wrT, woT, prm, s1, scr_r, scr_ry)
    k.ctx.close()
    return nc




E_EXP = 16
CAP = 256
DFF = 2048


def emit_moe(k: Ker, s1_tm, rw1, rw2, w1, w2, w3, s2_tm, acc):
    nc = k.nc
    ident = k.sb("mo_id", [128, 128], BF16)
    iot = k.sb("mo_iot", [128, 128], F32)
    logits = k.sb("mo_lg", [64, T], F32)
    vals = k.sb("mo_vals", [64, CAP], F32)
    idx = k.sb("mo_idx", [64, CAP], U32)
    m8 = k.sb("mo_m8", [64, 8], F32)
    psel = k.sb("mo_psel", [64, CAP], F32)
    wsel = k.sb("mo_wsel", [64, CAP], F32)
    rcol = k.sb("mo_rcol", [64, 8], F32)
    idxT = k.sb("mo_idxT", [128, 2, 64], U32)
    wselT = k.sb("mo_wselT", [128, 2, 64], F32)
    rw1sb = k.sb("mo_rw1", [128, NCH, 128], BF16)
    rw2sb = k.sb("mo_rw2", [128, 16], BF16)
    resid = k.sb("mo_res", [128, D], F32)
    atile = k.sb("mo_at", [128, D], F32)
    stile = k.sb("mo_sti", [128, D], F32)
    junk = k.sb("mo_jk", [128, D], F32)
    ocol = k.sb("mo_oc", [128, 8], F32)

    psA = k.ps("mo_psA", [128, 4, 512], F32)
    psB = k.ps("mo_psB", [128, 4, 512], F32)
    psAb = psA.ap().bitcast(BF16)

    ld = k.sem("ld"); st = k.sem("st"); tc = k.sem("tc")
    vc = k.sem("vc"); sc = k.sem("sc")
    ga = k.sem("ga")
    wl = [k.sem("wl0"), k.sem("wl1")]
    w3l = [k.sem("w3l0"), k.sem("w3l1")]
    sa = [k.sem("sa0"), k.sem("sa1")]

    hfm_ctx = ExitStack()
    hfm = hfm_ctx.enter_context(nc.sbuf_tensor("mo_hfm", [128, NCH, T], BF16))
    l1b = hfm_ctx.enter_context(nc.sbuf_tensor("mo_l1", [128, T], BF16))
    ztile = hfm_ctx.enter_context(nc.sbuf_tensor("mo_z", [128, 2 * D], F32))
    curA = hfm_ctx.enter_context(nc.sbuf_tensor("mo_curA", [64, T], F32))
    curB = hfm_ctx.enter_context(nc.sbuf_tensor("mo_curB", [64, T], F32))
    junk2 = hfm_ctx.enter_context(nc.sbuf_tensor("mo_junk2", [64, T], F32))

    ga0_ = ga.n
    with nc.Block(no_gpsimd_drain=True) as b:
        @b.gpsimd
        def _(g):
            g.iota(iot[:, :], [[1, 128]], base=0, channel_multiplier=-1,
                   allow_small_or_imprecise_dtypes=True)
            fdma(ga, g.dma_start(out=rw1sb[:, :, :],
                                 in_=rw1.rearrange("(ci p) h -> p ci h", p=128)))
            fdma(ga, g.dma_start(out=rw2sb[:, :], in_=rw2))
        @b.vector
        def _(v):
            v.memset(ocol[:, 0:1], 1e-5)
            v.memset(ztile[:, :], 0.0)
    z0 = st.n
    with nc.Block(no_gpsimd_drain=True) as b:
        @b.vector
        def _(v):
            v.tensor_scalar(ident[:, :], iot[:, :], 0.0, None, ALU.is_equal)
        @b.scalar
        def _(s):
            for bi in range(BL):
                for qd in range(8):
                    fdma(st, s.dma_start(
                        out=acc[bi, qd * 256:(qd + 1) * 256, :]
                            .rearrange("(q p) d -> p q d", p=128),
                        in_=ztile.ap().rearrange("p (q d) -> p q d", q=2)))
    ZDONE = st.n

    for bi in range(BL):
        l0 = ld.n; t0 = tc.n; v0 = vc.n
        with nc.Block(no_gpsimd_drain=True) as b:
            @b.sync
            def _(sy):
                for tci in range(16):
                    if tci > 0:
                        sy.wait_ge(vc.h, v0 + 2 * tci - 1)
                    fdma(ld, sy.dma_start(out=junk[:, :],
                                          in_=s1_tm[bi, tci * 128:(tci + 1) * 128, :]))
            @b.vector
            def _(v):
                for tci in range(16):
                    v.wait_ge(ld.h, l0 + 16 * (tci + 1))
                    fin(vc, v.tensor_copy(stile.ap().bitcast(BF16)[:, 0:D],
                                          junk[:, :]))
                    v.wait_ge(tc.h, t0 + 8 * (tci + 1))
                    fin(vc, v.tensor_copy(hfm[:, :, tci * 128:(tci + 1) * 128],
                                          psAb[:, 0, :]
                                          .rearrange("p (a b) -> p a b", a=8)))
            @b.tensor
            def _(t):
                for tci in range(16):
                    t.wait_ge(vc.h, v0 + 2 * tci + 1)
                    for ci in range(NCH):
                        fin(tc, t.transpose(
                            psAb[:, 0, ci * 128:(ci + 1) * 128],
                            stile.ap().bitcast(BF16)[:, ci * 128:(ci + 1) * 128],
                            ident[:, :]))
        t0 = tc.n; c0 = sc.n
        with nc.Block(no_gpsimd_drain=True) as b:
            @b.tensor
            def _(t):
                t.wait_ge(ga.h, ga0_ + 32)
                for tp in range(4):
                    for ci in range(NCH):
                        mm = t.matmul(psA[:, tp, :], rw1sb[:, ci, :],
                                      hfm[:, ci, tp * 512:(tp + 1) * 512],
                                      start=(ci == 0), stop=(ci == NCH - 1))
                    fin(tc, mm)
            @b.scalar
            def _(s):
                s.wait_ge(tc.h, t0 + 4)
                fin(sc, s.activation(l1b[:, :],
                                     psA.ap().rearrange("p a b -> p (a b)"), AF.Silu))
        t0 = tc.n; c0 = sc.n
        base = 32 * bi
        with nc.Block(no_gpsimd_drain=True) as b:
            @b.tensor
            def _(t):
                t.wait_ge(sc.h, c0)
                for tp in range(4):
                    mm = t.matmul(psB[base:base + 16, tp, :], rw2sb[:, :],
                                  l1b[:, tp * 512:(tp + 1) * 512],
                                  start=True, stop=True)
                    fin(tc, mm)
            @b.vector
            def _(v):
                if bi == 0:
                    v.memset(logits[:, :], -1e30)
                v.wait_ge(tc.h, t0 + 4)
                v.tensor_copy(logits[base:base + 16, :],
                              psB[base:base + 16, :, :].rearrange("p a b -> p (a b)"))
    with nc.Block(no_gpsimd_drain=True) as b:
        @b.vector
        def _(v):
            cur = logits
            nxt = curA
            for r in range(32):
                t1 = fin(vc, v.max(out=m8[:, :], in_=cur[:, :]))
                v.wait_ge(vc.h, t1)
                t2 = fin(vc, v.max_index(idx[:, 8 * r:8 * r + 8], m8[:, :], cur[:, :]))
                t3 = fin(vc, v.tensor_copy(vals[:, 8 * r:8 * r + 8], m8[:, :]))
                t4 = fin(vc, v.match_replace(out=nxt[:, :], in_to_replace=m8[:, :],
                                             in_values=cur[:, :], imm_value=-1e30))
                v.wait_ge(vc.h, t4)
                cur, nxt = nxt, (curB if nxt is curA else curA)
    v0 = vc.n; c0 = sc.n
    with nc.Block(no_gpsimd_drain=True) as b:
        @b.vector
        def _(v):
            t1 = fin(vc, v.tensor_scalar(rcol[:, 0:1], vals[:, 0:1], -1.0, None,
                                         ALU.mult))
            v.wait_ge(vc.h, t1)
        @b.scalar
        def _(s):
            s.wait_ge(vc.h, v0 + 1)
            fin(sc, s.activation(junk2[:, :], logits[:, :], AF.Exp,
                                 bias=rcol[:, 0:1], accum_out=rcol[:, 1:2]))
            s.wait_ge(sc.h, sc.n)
            fin(sc, s.activation(psel[:, :], vals[:, :], AF.Exp,
                                 bias=rcol[:, 0:1], accum_out=rcol[:, 2:3]))
    v0 = vc.n
    with nc.Block(no_gpsimd_drain=True) as b:
        @b.vector
        def _(v):
            t1 = fin(vc, v.scalar_tensor_tensor(rcol[:, 3:4], rcol[:, 1:2], 1e-9,
                                                rcol[:, 2:3], ALU.mult, ALU.add))
            v.wait_ge(vc.h, t1)
            t2 = fin(vc, v.reciprocal(rcol[:, 4:5], rcol[:, 3:4]))
            v.wait_ge(vc.h, t2)
            t3 = fin(vc, v.tensor_scalar(wsel[:, :], psel[:, :], rcol[:, 4:5], None,
                                         ALU.mult))
            v.wait_ge(vc.h, t3)
            for h in range(2):
                for pb in range(4):
                    for rb in range(2):
                        fin(vc, v.transpose(
                            idxT.ap().bitcast(F32)[32 * pb:32 * pb + 32, h,
                                                   32 * rb:32 * rb + 32],
                            idx.ap().bitcast(F32)[32 * rb:32 * rb + 32,
                                                  32 * (h * 4 + pb):32 * (h * 4 + pb) + 32]))
                        fin(vc, v.transpose(
                            wselT[32 * pb:32 * pb + 32, h, 32 * rb:32 * rb + 32],
                            wsel[32 * rb:32 * rb + 32,
                                 32 * (h * 4 + pb):32 * (h * 4 + pb) + 32]))
            v.wait_ge(vc.h, vc.n)
    RDONE = vc.n
    hfm_ctx.close()

    gb4 = k.sb("mo_gb4", [128, 4, D], BF16)
    ein = k.sb("mo_ein", [128, NCH, 2 * CAP], BF16)
    w1h = [k.sb(f"mo_w1h{i}", [128, NCH, DFF // 2], BF16) for i in range(2)]
    w2h = [k.sb(f"mo_w2h{i}", [128, NCH, DFF // 2], BF16) for i in range(2)]
    w3f = k.sb("mo_w3f", [128, 2 * NCH, D], BF16)
    vq = k.sb("mo_vq", [128, 4, 512], BF16)
    sg = k.sb("mo_sg", [128, 4, 512], BF16)
    gv = k.sb("mo_gv", [128, E_EXP, 512], BF16)
    wtd = [k.sb(f"mo_wtd{i}", [128, 4, D], F32) for i in range(2)]
    with nc.Block(no_gpsimd_drain=True) as b:
        @b.gpsimd
        def _(g):
            fdma(wl[0], g.dma_start(out=w1h[0][:, :, :],
                                 in_=w1[0, :, 0:DFF // 2]
                                 .rearrange("(ci p) f -> p ci f", p=128)))
            fdma(wl[0], g.dma_start(out=w2h[0][:, :, :],
                                 in_=w2[0, :, 0:DFF // 2]
                                 .rearrange("(ci p) f -> p ci f", p=128)))
            fdma(w3l[0], g.dma_start(out=w3f[:, :, :],
                                 in_=w3[0].rearrange("(ci p) d -> p ci d", p=128)))

    for e in range(E_EXP):
        g0 = ga.n; t0 = tc.n; v0 = vc.n; c0 = sc.n
        with nc.Block(no_gpsimd_drain=True) as b:
            @b.gpsimd
            def _(g):
                g.wait_ge(vc.h, RDONE)
                for bi in range(BL):
                    for h in range(2):
                        fdma(ga, g.indirect_dma_start(
                            out=gb4[:, bi * 2 + h, :], out_offset=None,
                            in_=s1_tm.rearrange("b t d -> (b t) d"),
                            in_offset=bass.IndirectOffsetOnAxis(
                                ap=idxT[:, h, 32 * bi + e:32 * bi + e + 1], axis=0),
                            element_offset=bi * T * D))
                off = DFF // 2
                fdma(wl[1], g.dma_start(out=w1h[1][:, :, :],
                                        in_=w1[e, :, off:off + DFF // 2]
                                        .rearrange("(ci p) f -> p ci f", p=128)))
                fdma(wl[1], g.dma_start(out=w2h[1][:, :, :],
                                        in_=w2[e, :, off:off + DFF // 2]
                                        .rearrange("(ci p) f -> p ci f", p=128)))
                if e + 1 < E_EXP:
                    g.wait_ge(tc.h, t0 + 8)
                    fdma(wl[0], g.dma_start(out=w1h[0][:, :, :],
                                            in_=w1[e + 1, :, 0:DFF // 2]
                                            .rearrange("(ci p) f -> p ci f", p=128)))
                    fdma(wl[0], g.dma_start(out=w2h[0][:, :, :],
                                            in_=w2[e + 1, :, 0:DFF // 2]
                                            .rearrange("(ci p) f -> p ci f", p=128)))
                    g.wait_ge(tc.h, t0 + 20)
                    fdma(w3l[0], g.dma_start(
                        out=w3f[:, :, :],
                        in_=w3[e + 1].rearrange("(ci p) d -> p ci d", p=128)))
                g.wait_ge(vc.h, v0 + 20)
                g.wait_ge(st.h, ZDONE)
                for bi in range(BL):
                    g.wait_ge(sa[bi].h, 32 * e)
                    for h in range(2):
                        fdma(sa[bi], g.indirect_dma_start(
                            out=acc.rearrange("b t d -> (b t) d"),
                            out_offset=bass.IndirectOffsetOnAxis(
                                ap=idxT[:, h, 32 * bi + e:32 * bi + e + 1], axis=0),
                            in_=wtd[e % 2][:, bi * 2 + h, :], in_offset=None,
                            compute_op=ALU.add, element_offset=bi * T * D))

            @b.tensor
            def _(t):
                for cc in range(4):
                    t.wait_ge(ga.h, g0 + 64)
                    if cc > 0:
                        t.wait_ge(vc.h, v0 + cc)
                    for ci in range(NCH):
                        mm = t.transpose(psAb[:, 0, ci * 128:(ci + 1) * 128],
                                         gb4[:, cc, ci * 128:(ci + 1) * 128],
                                         ident[:, :])
                    fin(tc, mm)
                t.wait_ge(vc.h, v0 + 4)
                for h in range(2):
                    par = h
                    t.wait_ge(wl[par].h, 32 * (e + 1))
                    for q in range(2):
                        for dc in range(4):
                            for ci in range(NCH):
                                mm = t.matmul(psA[:, dc, :],
                                              w2h[par][:, ci, (q * 4 + dc) * 128:
                                                       (q * 4 + dc + 1) * 128],
                                              ein[:, ci, :],
                                              start=(ci == 0), stop=(ci == NCH - 1))
                        fin(tc, mm)
                        t.wait_ge(vc.h, v0 + 4 + 4 * h + 2 * q + 1)
                        for dc in range(4):
                            for ci in range(NCH):
                                mm = t.matmul(psA[:, dc, :],
                                              w1h[par][:, ci, (q * 4 + dc) * 128:
                                                       (q * 4 + dc + 1) * 128],
                                              ein[:, ci, :],
                                              start=(ci == 0), stop=(ci == NCH - 1))
                        fin(tc, mm)
                        t.wait_ge(sc.h, c0 + 2 * h + q + 1)
                t.wait_ge(w3l[0].h, 16 * (e + 1))
                t.wait_ge(vc.h, v0 + 12)
                for dh in range(2):
                    if dh > 0:
                        t.wait_ge(vc.h, v0 + 16)
                    for cc in range(4):
                        for ci in range(2 * NCH):
                            mm = t.matmul(psB[:, cc, :],
                                          gv[:, ci, cc * 128:(cc + 1) * 128],
                                          w3f[:, ci, dh * 512:(dh + 1) * 512],
                                          start=(ci == 0), stop=(ci == 2 * NCH - 1))
                        fin(tc, mm)

            @b.vector
            def _(v):
                for cc in range(4):
                    v.wait_ge(tc.h, t0 + cc + 1)
                    fin(vc, v.tensor_copy(
                        ein[:, :, cc * 128:(cc + 1) * 128],
                        psAb[:, 0, :].rearrange("p (a b) -> p a b", a=8)))
                for h in range(2):
                    for q in range(2):
                        v.wait_ge(tc.h, t0 + 4 + 4 * h + 2 * q + 1)
                        fin(vc, v.tensor_copy(vq.ap().rearrange("p a b -> p (a b)"),
                                              psA.ap().rearrange("p a b -> p (a b)")))
                        v.wait_ge(sc.h, c0 + 2 * h + q + 1)
                        fin(vc, v.tensor_tensor(
                            gv[:, h * 8 + q * 4:h * 8 + q * 4 + 4, :],
                            sg.ap(), vq.ap(), ALU.mult))
                if e >= 2:
                    v.wait_ge(sa[0].h, 32 * (e - 1))
                    v.wait_ge(sa[1].h, 32 * (e - 1))
                for dh in range(2):
                    for cc in range(4):
                        v.wait_ge(tc.h, t0 + 12 + dh * 4 + cc + 1)
                        bi, hh = cc // 2, cc % 2
                        fin(vc, v.tensor_scalar(
                            wtd[e % 2][:, cc, dh * 512:(dh + 1) * 512],
                            psB[:, cc, :],
                            wselT[:, hh, 32 * bi + e:32 * bi + e + 1], None,
                            ALU.mult))

            @b.scalar
            def _(s):
                for h in range(2):
                    for q in range(2):
                        s.wait_ge(tc.h, t0 + 4 + 4 * h + 2 * q + 2)
                        fin(sc, s.activation(sg.ap().rearrange("p a b -> p (a b)"),
                                             psA.ap().rearrange("p a b -> p (a b)"),
                                             AF.Silu))

    with nc.Block(no_gpsimd_drain=True) as b:
        @b.gpsimd
        def _(g):
            g.wait_ge(sa[0].h, 32 * E_EXP)
            g.wait_ge(sa[1].h, 32 * E_EXP)
    for bi in range(BL):
        for tci in range(16):
            l0 = ld.n
            with nc.Block(no_gpsimd_drain=True) as b:
                @b.sync
                def _(sy):
                    fdma(ld, sy.dma_start(out=atile[:, :],
                                          in_=acc[bi, tci * 128:(tci + 1) * 128, :]))
                    fdma(ld, sy.dma_start(out=resid[:, :],
                                          in_=s1_tm[bi, tci * 128:(tci + 1) * 128, :]))
                @b.vector
                def _(v):
                    v.wait_ge(ld.h, l0 + 32)
            _ln_chunk(k, atile[:, :], resid, stile, junk, ocol, k.sems["vc"],
                      k.sems["sc"],
                      store=(s2_tm[bi, tci * 128:(tci + 1) * 128, :], st))


def moe_host_inputs(core, inputs, s1):
    return {
        **({"s1_tm": np.ascontiguousarray(s1[core * BL:(core + 1) * BL])}
           if s1 is not None else {}),
        "rw1": np.ascontiguousarray(np.asarray(inputs["L0_ffn_router_w1"])),
        "rw2": np.ascontiguousarray(np.asarray(inputs["L0_ffn_router_w2"])),
        "w1": np.ascontiguousarray(np.asarray(inputs["L0_ffn_w1"])),
        "w2": np.ascontiguousarray(np.asarray(inputs["L0_ffn_w2"])),
        "w3": np.ascontiguousarray(np.asarray(inputs["L0_ffn_w3"])),
    }


def build_moe_program():
    nc = bass.Bass("TRN2", target_bir_lowering=False)
    k = Ker(nc)
    s1 = nc.dram_tensor("s1_tm", [BL, T, D], F32, kind="ExternalInput").ap()
    rw1 = nc.dram_tensor("rw1", [D, 128], F32, kind="ExternalInput").ap()
    rw2 = nc.dram_tensor("rw2", [128, 16], F32, kind="ExternalInput").ap()
    w1 = nc.dram_tensor("w1", [E_EXP, D, DFF], F32, kind="ExternalInput").ap()
    w2 = nc.dram_tensor("w2", [E_EXP, D, DFF], F32, kind="ExternalInput").ap()
    w3 = nc.dram_tensor("w3", [E_EXP, DFF, D], F32, kind="ExternalInput").ap()
    s2 = nc.dram_tensor("s2_tm", [BL, T, D], F32, kind="ExternalOutput").ap()
    acc = nc.dram_tensor("acc", [BL, T, D], F32).ap()
    emit_moe(k, s1, rw1, rw2, w1, w2, w3, s2, acc)
    k.ctx.close()
    return nc


HD = 4
DK = 256
CHK = 128


def emit_l1a(k: Ker, s2_tm, wq, wk, wv, wb, wo, gnorm, s3_tm):
    import os
    STOP = int(os.environ.get("L1A_STOP", "99"))
    nc = k.nc
    iot = k.sb("la_iot", [128, 128], F32)
    identb = k.sb("la_idb", [128, 128], BF16)
    maskSL = k.sb("la_msl", [128, 128], BF16)
    maskUI = k.sb("la_mui", [128, 128], BF16)
    gtile = k.sb("la_gt", [128, DK], F32)
    grow = k.sb("la_gr", [1, DK], F32)
    growb = k.sb("la_grb", [1, DK], BF16)
    ones1 = k.sb("la_o1", [1, 128], BF16)
    wqs = k.sb("la_wq", [128, NCH, D], BF16)
    wks = k.sb("la_wk", [128, NCH, D], BF16)
    wvs = k.sb("la_wv", [128, NCH, D], BF16)
    wbs = k.sb("la_wb", [128, NCH, 4], BF16)
    wos = k.sb("la_wo", [128, NCH, D], BF16)
    xfm = k.sb("la_xfm", [128, NCH, T], BF16)
    ofm = k.sb("la_ofm", [128, NCH, T], BF16)
    S32 = k.sb("la_S32", [128, 2, HD * DK], F32)
    Sbf = k.sb("la_Sbf", [128, 2, HD * DK], BF16)
    ctile = k.sb("la_ct", [128, D], BF16)
    ftile = k.sb("la_ft", [128, D], F32)
    qn = k.sb("la_qn", [128, D], BF16)
    kn = k.sb("la_kn", [128, D], BF16)
    vch = k.sb("la_vch", [128, D], BF16)
    qfmc = k.sb("la_qfm", [128, NCH, CHK], BF16)
    kfmc = k.sb("la_kfm", [128, NCH, CHK], BF16)
    beta = k.sb("la_beta", [128, 8], F32)
    Bbuf = k.sb("la_B", [128, HD, CHK], BF16)
    Pbuf = k.sb("la_P", [128, HD, CHK], BF16)
    Btr = k.sb("la_Bt", [128, HD, CHK], BF16)
    Ptr = k.sb("la_Pt", [128, HD, CHK], BF16)
    MKQ = k.sb("la_mkq", [128, HD, CHK], BF16)
    Ebuf = k.sb("la_E", [128, HD, DK], BF16)
    TEb = k.sb("la_TE", [128, HD, DK], BF16)
    tmpf = k.sb("la_tmp", [128, D], F32)
    junk = k.sb("la_jk", [128, D], F32)
    ncol = k.sb("la_nc", [128, 8], F32)
    ncol2 = k.sb("la_nc2", [128, 8], F32)
    bcol = k.sb("la_bc", [128, 8], F32)
    onec = k.sb("la_onec", [128, 1], F32)
    resid = k.sb("la_res", [128, D], F32)
    stile = k.sb("la_sti", [128, D], F32)
    ocol = k.sb("la_oc", [128, 8], F32)

    psA = k.ps("la_psA", [128, 4, 512], F32)
    psB = k.ps("la_psB", [128, 4, 512], F32)
    psAb = psA.ap().bitcast(BF16)
    psBb = psB.ap().bitcast(BF16)

    ld = k.sem("ld"); st = k.sem("st"); tc = k.sem("tc")
    vc = k.sem("vc"); sc = k.sem("sc"); gl = k.sem("gl")

    g0 = gl.n; l0 = ld.n
    with nc.Block(no_gpsimd_drain=True) as b:
        @b.gpsimd
        def _(g):
            g.iota(iot[:, :], [[1, 128]], base=0, channel_multiplier=-1,
                   allow_small_or_imprecise_dtypes=True)
            fdma(gl, g.dma_start(out=wqs[:, :, :],
                                 in_=wq.rearrange("(ci p) o -> p ci o", p=128)))
            fdma(gl, g.dma_start(out=wks[:, :, :],
                                 in_=wk.rearrange("(ci p) o -> p ci o", p=128)))
            fdma(gl, g.dma_start(out=wvs[:, :, :],
                                 in_=wv.rearrange("(ci p) o -> p ci o", p=128)))
            fdma(gl, g.dma_start(out=wbs[:, :, :],
                                 in_=wb.rearrange("(ci p) o -> p ci o", p=128)))
            fdma(gl, g.dma_start(out=wos[:, :, :],
                                 in_=wo.rearrange("(ci p) o -> p ci o", p=128)))
        @b.sync
        def _(sy):
            fdma(ld, sy.dma_start(out=grow[:, :], in_=gnorm.unsqueeze(0)))
    k.l1a_wdone = gl.n
    with nc.Block(no_gpsimd_drain=True) as b:
        @b.vector
        def _(v):
            v.memset(ocol[:, 0:1], 1e-5)
            v.memset(onec[:, 0:1], 1.0)
            v.memset(ones1[:, :], 1.0)
            v.tensor_scalar(maskSL[:, :], iot[:, :], 0.0, None, ALU.is_lt)
            v.tensor_scalar(maskUI[:, :], iot[:, :], 0.0, None, ALU.is_ge)
            v.tensor_scalar(identb[:, :], iot[:, :], 0.0, None, ALU.is_equal)
            v.wait_ge(ld.h, l0 + 16)
            fin(vc, v.tensor_copy(growb[:, :], grow[:, :]))
    t0 = tc.n
    with nc.Block(no_gpsimd_drain=True) as b:
        @b.tensor
        def _(t):
            fin(tc, t.matmul(psA[:, 0, 0:DK], ones1[:, :], growb[:, :],
                             start=True, stop=True))
        @b.vector
        def _(v):
            v.wait_ge(tc.h, t0 + 1)
            v.tensor_copy(gtile[:, :], psA[:, 0, 0:DK])

    for bi in range(BL):
        li = ld.n; t0 = tc.n; v0 = vc.n
        with nc.Block(no_gpsimd_drain=True) as b:
            @b.sync
            def _(sy):
                for tci in range(16):
                    if tci > 0:
                        sy.wait_ge(vc.h, v0 + 2 * tci - 1)
                    fdma(ld, sy.dma_start(out=ftile[:, :],
                                          in_=s2_tm[bi, tci * 128:(tci + 1) * 128, :]))
            @b.vector
            def _(v):
                for tci in range(16):
                    v.wait_ge(ld.h, li + 16 * (tci + 1))
                    fin(vc, v.tensor_copy(ctile[:, :], ftile[:, :]))
                    v.wait_ge(tc.h, t0 + 8 * (tci + 1))
                    fin(vc, v.tensor_copy(xfm[:, :, tci * 128:(tci + 1) * 128],
                                          psAb[:, 0, :]
                                          .rearrange("p (a b) -> p a b", a=8)))
            @b.tensor
            def _(t):
                for tci in range(16):
                    t.wait_ge(vc.h, v0 + 2 * tci + 1)
                    for ci in range(NCH):
                        fin(tc, t.transpose(psAb[:, 0, ci * 128:(ci + 1) * 128],
                                            ctile[:, ci * 128:(ci + 1) * 128],
                                            identb[:, :]))
        with nc.Block(no_gpsimd_drain=True) as b:
            @b.vector
            def _(v):
                v.memset(S32[:, :, :], 0.0)
                v.memset(Sbf[:, :, :], 0.0)

        for m in range(16):
            t0 = tc.n; c0 = sc.n; v0 = vc.n
            TQ, TK, TV, TB = t0 + 1, t0 + 2, t0 + 3, t0 + 4
            TQT, TKT = t0 + 12, t0 + 20
            TKQ, TKK, TBT = t0 + 24, t0 + 28, t0 + 32
            TST = t0 + 32
            TPT, TTE, TO = t0 + 108, t0 + 116, t0 + 120
            TCT, TSU = t0 + 128, t0 + 136
            V_QR, V_KR = v0 + 1, v0 + 2
            V_QS, V_KS, V_VC, V_NB = v0 + 6, v0 + 10, v0 + 11, v0 + 12
            V_QF, V_KF = v0 + 13, v0 + 14
            V_MKQ, V_BB = v0 + 18, v0 + 26
            V_BT0 = v0 + 31
            V_PT = v0 + 50
            V_EB, V_TE = v0 + 58, v0 + 59
            V_OR, V_CT, V_OF = v0 + 60, v0 + 64, v0 + 65
            C_QSQ, C_KSQ, C_B3 = c0 + 1, c0 + 2, c0 + 5
            C_QE, C_KE = c0 + 7, c0 + 9
            C_OSQ, C_OE = c0 + 10, c0 + 12
            with nc.Block(no_gpsimd_drain=True) as b:
                @b.tensor
                def _(t):
                    t.wait_ge(gl.h, k.l1a_wdone)
                    for half in range(2):
                        for ci in range(NCH):
                            mm = t.matmul(psA[:, half, :],
                                          xfm[:, ci, m * 128:m * 128 + 128],
                                          wqs[:, ci, half * 512:(half + 1) * 512],
                                          start=(ci == 0), stop=(ci == NCH - 1))
                    fin(tc, mm)
                    for half in range(2):
                        for ci in range(NCH):
                            mm = t.matmul(psB[:, half, :],
                                          xfm[:, ci, m * 128:m * 128 + 128],
                                          wks[:, ci, half * 512:(half + 1) * 512],
                                          start=(ci == 0), stop=(ci == NCH - 1))
                    fin(tc, mm)
                    for half in range(2):
                        for ci in range(NCH):
                            mm = t.matmul(psA[:, 2 + half, :],
                                          xfm[:, ci, m * 128:m * 128 + 128],
                                          wvs[:, ci, half * 512:(half + 1) * 512],
                                          start=(ci == 0), stop=(ci == NCH - 1))
                    fin(tc, mm)
                    for ci in range(NCH):
                        mm = t.matmul(psB[:, 2, 0:4], xfm[:, ci, m * 128:m * 128 + 128],
                                      wbs[:, ci, :], start=(ci == 0), stop=(ci == NCH - 1))
                    fin(tc, mm)
                    t.wait_ge(vc.h, V_QS)
                    for ci in range(NCH):
                        fin(tc, t.transpose(psAb[:, 0, ci * 128:(ci + 1) * 128],
                                            qn[:, ci * 128:(ci + 1) * 128], identb[:, :]))
                    t.wait_ge(vc.h, V_KS)
                    for ci in range(NCH):
                        fin(tc, t.transpose(psBb[:, 3, ci * 128:(ci + 1) * 128],
                                            kn[:, ci * 128:(ci + 1) * 128], identb[:, :]))
                    t.wait_ge(vc.h, V_KF)
                    for h in range(HD):
                        for dc in range(2):
                            mm = t.matmul(psA[:, h, 0:128],
                                          kfmc[:, 2 * h + dc, :], qfmc[:, 2 * h + dc, :],
                                          start=(dc == 0), stop=(dc == 1))
                        fin(tc, mm)
                    for h in range(HD):
                        for dc in range(2):
                            mm = t.matmul(psA[:, h, 128:256],
                                          kfmc[:, 2 * h + dc, :], kfmc[:, 2 * h + dc, :],
                                          start=(dc == 0), stop=(dc == 1))
                        fin(tc, mm)
                    t.wait_ge(vc.h, V_BB)
                    for h in range(HD):
                        fin(tc, t.transpose(psAb[:, h, 0:128], Bbuf[:, h, :],
                                            identb[:, :]))
                    for stg in range(6):
                        t.wait_ge(vc.h, V_BT0 + 3 * stg)
                        for h in range(HD):
                            fin(tc, t.matmul(psB[:, h, 0:128], Btr[:, h, :],
                                             Bbuf[:, h, :], start=True, stop=True))
                        for h in range(HD):
                            fin(tc, t.matmul(psB[:, h, 128:256], Bbuf[:, h, :],
                                             Btr[:, h, :], start=True, stop=True))
                        t.wait_ge(vc.h, V_BT0 + 3 * stg + 2)
                        for h in range(HD):
                            fin(tc, t.matmul(psA[:, h, 0:128], Btr[:, h, :],
                                             Pbuf[:, h, :], start=True, stop=True))
                    t.wait_ge(vc.h, V_BT0 + 18)
                    for h in range(HD):
                        fin(tc, t.transpose(psAb[:, h, 0:128], Pbuf[:, h, :],
                                            identb[:, :]))
                    t.wait_ge(vc.h, V_PT)
                    for h in range(HD):
                        for dc in range(2):
                            mm = t.matmul(psA[:, h, 0:256],
                                          kfmc[:, 2 * h + dc, :],
                                          Sbf[:, dc, h * DK:(h + 1) * DK],
                                          start=(dc == 0), stop=(dc == 1))
                        fin(tc, mm)
                    t.wait_ge(vc.h, V_EB)
                    for h in range(HD):
                        fin(tc, t.matmul(psA[:, h, 0:256], Ptr[:, h, :], Ebuf[:, h, :],
                                         start=True, stop=True))
                    t.wait_ge(vc.h, V_TE)
                    for h in range(HD):
                        for dc in range(2):
                            t.matmul(psA[:, h, 0:256],
                                     qfmc[:, 2 * h + dc, :],
                                     Sbf[:, dc, h * DK:(h + 1) * DK],
                                     start=(dc == 0), stop=False)
                        mm = t.matmul(psA[:, h, 0:256], MKQ[:, h, :], TEb[:, h, :],
                                      start=False, stop=True)
                        fin(tc, mm)
                    t.wait_ge(vc.h, V_CT)
                    for ci in range(NCH):
                        fin(tc, t.transpose(psBb[:, 0, ci * 128:(ci + 1) * 128],
                                            ctile[:, ci * 128:(ci + 1) * 128],
                                            identb[:, :]))
                    t.wait_ge(vc.h, V_OF)
                    for dc in range(2):
                        for h in range(HD):
                            fin(tc, t.matmul(
                                psB[:, 2 * dc + h // 2, (h % 2) * 256:(h % 2) * 256 + 256],
                                kn[:, h * DK + dc * 128:h * DK + dc * 128 + 128],
                                TEb[:, h, :], start=True, stop=True))

                @b.scalar
                def _(s):
                    s.wait_ge(tc.h, TQ)
                    fin(sc, s.activation(junk[:, :],
                                         psA[:, 0:2, :].rearrange("p a b -> p (a b)"),
                                         AF.Square))
                    s.wait_ge(tc.h, TK)
                    fin(sc, s.activation(tmpf[:, :],
                                         psB[:, 0:2, :].rearrange("p a b -> p (a b)"),
                                         AF.Square))
                    s.wait_ge(tc.h, TB)
                    b1 = fin(sc, s.activation(bcol[:, 0:4], psB[:, 2, 0:4], AF.Exp,
                                              scale=-1.0))
                    s.wait_ge(sc.h, b1)
                    b2 = fin(sc, s.activation(bcol[:, 4:8], bcol[:, 0:4], AF.Ln,
                                              bias=onec[:, 0:1], scale=1.0))
                    s.wait_ge(sc.h, b2)
                    fin(sc, s.activation(beta[:, 0:4], bcol[:, 4:8], AF.Exp, scale=-1.0))
                    s.wait_ge(vc.h, V_QR)
                    q1 = fin(sc, s.activation(ncol[:, 4:8], ncol[:, 0:4], AF.Ln))
                    s.wait_ge(sc.h, q1)
                    fin(sc, s.activation(ncol[:, 0:4], ncol[:, 4:8], AF.Exp, scale=-0.5))
                    s.wait_ge(vc.h, V_KR)
                    k1 = fin(sc, s.activation(ncol2[:, 4:8], ncol2[:, 0:4], AF.Ln))
                    s.wait_ge(sc.h, k1)
                    fin(sc, s.activation(ncol2[:, 0:4], ncol2[:, 4:8], AF.Exp,
                                         scale=-0.5))
                    s.wait_ge(tc.h, TO)
                    fin(sc, s.activation(junk.ap().rearrange("p (a b) -> p a b", a=4),
                                         psA[:, :, 0:256], AF.Square))
                    s.wait_ge(vc.h, V_OR)
                    o1 = fin(sc, s.activation(ncol[:, 4:8], ncol[:, 0:4], AF.Ln,
                                              bias=ocol[:, 0:1], scale=1.0 / DK))
                    s.wait_ge(sc.h, o1)
                    fin(sc, s.activation(ncol[:, 0:4], ncol[:, 4:8], AF.Exp, scale=-0.5))

                @b.vector
                def _(v):
                    v.wait_ge(sc.h, C_QSQ)
                    fin(vc, v.tensor_reduce(ncol[:, 0:4],
                                            junk.ap().rearrange("p (a b) -> p a b", a=4),
                                            axis=AX.X, op=ALU.add))
                    v.wait_ge(sc.h, C_KSQ)
                    fin(vc, v.tensor_reduce(ncol2[:, 0:4],
                                            tmpf.ap().rearrange("p (a b) -> p a b", a=4),
                                            axis=AX.X, op=ALU.add))
                    v.wait_ge(sc.h, C_QE)
                    for h in range(HD):
                        fin(vc, v.tensor_scalar(qn[:, h * DK:(h + 1) * DK],
                                                psA[:, 0:2, :]
                                                .rearrange("p a b -> p (a b)")
                                                [:, h * DK:(h + 1) * DK],
                                                ncol[:, h:h + 1], None, ALU.mult))
                    v.wait_ge(sc.h, C_KE)
                    for h in range(HD):
                        fin(vc, v.tensor_scalar(kn[:, h * DK:(h + 1) * DK],
                                                psB[:, 0:2, :]
                                                .rearrange("p a b -> p (a b)")
                                                [:, h * DK:(h + 1) * DK],
                                                ncol2[:, h:h + 1], None, ALU.mult))
                    v.wait_ge(tc.h, TV)
                    fin(vc, v.tensor_copy(vch[:, :],
                                          psA[:, 2:4, :].rearrange("p a b -> p (a b)")))
                    v.wait_ge(sc.h, C_B3)
                    fin(vc, v.tensor_scalar(beta[:, 4:8], beta[:, 0:4], -1.0, None,
                                            ALU.mult))
                    v.wait_ge(tc.h, TQT)
                    fin(vc, v.tensor_copy(qfmc[:, :, :],
                                          psAb[:, 0, :]
                                          .rearrange("p (a b) -> p a b", a=8)))
                    v.wait_ge(tc.h, TKT)
                    fin(vc, v.tensor_copy(kfmc[:, :, :],
                                          psBb[:, 3, :]
                                          .rearrange("p (a b) -> p a b", a=8)))
                    v.wait_ge(tc.h, TKK)
                    for h in range(HD):
                        fin(vc, v.tensor_tensor(MKQ[:, h, :], psA[:, h, 0:128],
                                                maskUI[:, :], ALU.mult))
                    for h in range(HD):
                        t1 = fin(vc, v.tensor_scalar(tmpf[:, 0:128], psA[:, h, 128:256],
                                                     beta[:, 4 + h:5 + h], None,
                                                     ALU.mult))
                        v.wait_ge(vc.h, t1)
                        fin(vc, v.tensor_tensor(Bbuf[:, h, :], tmpf[:, 0:128],
                                                maskSL[:, :], ALU.mult))
                    for h in range(HD):
                        fin(vc, v.tensor_tensor(Pbuf[:, h, :], Bbuf[:, h, :],
                                                identb[:, :], ALU.add))
                    v.wait_ge(tc.h, TBT)
                    fin(vc, v.tensor_copy(Btr.ap(), psAb[:, :, 0:128]))
                    for stg in range(6):
                        v.wait_ge(tc.h, TST + 12 * stg + 8)
                        fin(vc, v.tensor_copy(Bbuf.ap(), psB[:, :, 0:128]))
                        fin(vc, v.tensor_copy(Btr.ap(), psB[:, :, 128:256]))
                        v.wait_ge(tc.h, TST + 12 * stg + 12)
                        t1 = fin(vc, v.tensor_tensor(Pbuf.ap(), Pbuf.ap(),
                                                     psA[:, :, 0:128], ALU.add))
                        v.wait_ge(vc.h, t1)
                    v.wait_ge(tc.h, TPT)
                    fin(vc, v.tensor_copy(Ptr.ap(), psAb[:, :, 0:128]))
                    for h in range(HD):
                        v.wait_ge(tc.h, TPT + h + 1)
                        t1 = fin(vc, v.tensor_tensor(tmpf[:, 0:DK],
                                                     vch[:, h * DK:(h + 1) * DK],
                                                     psA[:, h, 0:256], ALU.subtract))
                        v.wait_ge(vc.h, t1)
                        fin(vc, v.tensor_scalar(Ebuf[:, h, :], tmpf[:, 0:DK],
                                                beta[:, h:h + 1], None, ALU.mult))
                    v.wait_ge(tc.h, TTE)
                    fin(vc, v.tensor_copy(TEb.ap(), psA[:, :, 0:256]))
                    v.wait_ge(sc.h, C_OSQ)
                    fin(vc, v.tensor_reduce(ncol[:, 0:4],
                                            junk.ap().rearrange("p (a b) -> p a b", a=4),
                                            axis=AX.X, op=ALU.add))
                    v.wait_ge(sc.h, C_OE)
                    for h in range(HD):
                        fin(vc, v.scalar_tensor_tensor(ctile[:, h * DK:(h + 1) * DK],
                                                       psA[:, h, 0:256], ncol[:, h:h + 1],
                                                       gtile[:, :], ALU.mult, ALU.mult))
                    v.wait_ge(tc.h, TCT)
                    fin(vc, v.tensor_copy(ofm[:, :, m * 128:(m + 1) * 128],
                                          psBb[:, 0, :]
                                          .rearrange("p (a b) -> p a b", a=8)))
                    v.wait_ge(tc.h, TSU)
                    for dc in range(2):
                        t1 = fin(vc, v.tensor_tensor(S32[:, dc, :], S32[:, dc, :],
                                                     psB[:, 2 * dc:2 * dc + 2, :]
                                                     .rearrange("p a b -> p (a b)"),
                                                     ALU.add))
                        v.wait_ge(vc.h, t1)
                        fin(vc, v.tensor_copy(Sbf[:, dc, :], S32[:, dc, :]))

        if STOP <= 8: pass
        for tci in range(16):
            li = ld.n; t0 = tc.n
            with nc.Block(no_gpsimd_drain=True) as b:
                @b.sync
                def _(sy):
                    fdma(ld, sy.dma_start(out=resid[:, :],
                                          in_=s2_tm[bi, tci * 128:(tci + 1) * 128, :]))
                @b.tensor
                def _(t):
                    for half in range(2):
                        for ci in range(NCH):
                            mm = t.matmul(psA[:, half, :],
                                          ofm[:, ci, tci * 128:(tci + 1) * 128],
                                          wos[:, ci, half * 512:(half + 1) * 512],
                                          start=(ci == 0), stop=(ci == NCH - 1))
                    fin(tc, mm)
                @b.vector
                def _(v):
                    v.wait_ge(tc.h, t0 + 1)
                    v.wait_ge(ld.h, li + 16)
            _ln_chunk(k, psA[:, 0:2, :].rearrange("p a b -> p (a b)"), resid, stile,
                      junk, ocol, vc, sc,
                      store=(s3_tm[bi, tci * 128:(tci + 1) * 128, :], st))


def _l1a_proj_chunk(k, bi, m, xfm, wsb, out_tm, psA, junk, ncol, wbsb, do_norm,
                    tc, vc, sc, gl, beta=None):
    nc = k.nc
    t0 = tc.n; v0 = vc.n; c0 = sc.n
    with nc.Block(no_gpsimd_drain=True) as b:
        @b.tensor
        def _(t):
            t.wait_ge(gl.h, k.l1a_wdone)
            for half in range(2):
                for ci in range(NCH):
                    mm = t.matmul(psA[:, half, :],
                                  xfm[:, ci, m * 128:m * 128 + 128],
                                  wsb[:, ci, half * 512:(half + 1) * 512],
                                  start=(ci == 0), stop=(ci == NCH - 1))
            fin(tc, mm)
            if wbsb is not None:
                for ci in range(NCH):
                    mm = t.matmul(psA[:, 2, 0:4], xfm[:, ci, m * 128:m * 128 + 128],
                                  wbsb[:, ci, :], start=(ci == 0), stop=(ci == NCH - 1))
                fin(tc, mm)
        if not do_norm:
            @b.vector
            def _(v):
                v.wait_ge(tc.h, t0 + 1)
                fin(vc, v.tensor_copy(out_tm[:, :],
                                      psA[:, 0:2, :].rearrange("p a b -> p (a b)")))
            @b.scalar
            def _(s):
                s.wait_ge(tc.h, t0 + 2)
                t1 = fin(sc, s.activation(beta[:, 0:4], psA[:, 2, 0:4], AF.Sigmoid))
                s.wait_ge(sc.h, t1)
        else:
            @b.scalar
            def _(s):
                s.wait_ge(tc.h, t0 + 1)
                fin(sc, s.activation(junk[:, :],
                                     psA[:, 0:2, :].rearrange("p a b -> p (a b)"),
                                     AF.Square))
            @b.vector
            def _(v):
                v.wait_ge(sc.h, c0 + 1)
                t1 = fin(vc, v.tensor_reduce(ncol[:, 0:4],
                                             junk.ap().rearrange("p (a b) -> p a b", a=4),
                                             axis=AX.X, op=ALU.add))
                v.wait_ge(vc.h, t1)
    if do_norm:
        c0 = sc.n; v0 = vc.n
        with nc.Block(no_gpsimd_drain=True) as b:
            @b.scalar
            def _(s):
                t1 = fin(sc, s.activation(ncol[:, 4:8], ncol[:, 0:4], AF.Ln))
                s.wait_ge(sc.h, t1)
                fin(sc, s.activation(ncol[:, 0:4], ncol[:, 4:8], AF.Exp, scale=-0.5))
        with nc.Block(no_gpsimd_drain=True) as b:
            @b.vector
            def _(v):
                for h in range(HD):
                    fin(vc, v.tensor_scalar(out_tm[:, h * DK:(h + 1) * DK],
                                            psA[:, 0:2, :].rearrange("p a b -> p (a b)")
                                            [:, h * DK:(h + 1) * DK],
                                            ncol[:, h:h + 1], None, ALU.mult))
    else:
        v0 = vc.n
        with nc.Block(no_gpsimd_drain=True) as b:
            @b.vector
            def _(v):
                v.wait_ge(sc.h, sc.n)
                v.tensor_scalar(beta[:, 4:8], beta[:, 0:4], -1.0, None, ALU.mult)


def l1a_host_inputs(core, inputs, s2):
    return {
        **({"s2_tm": np.ascontiguousarray(s2[core * BL:(core + 1) * BL])}
           if s2 is not None else {}),
        "wq": np.ascontiguousarray(np.asarray(inputs["L1_attn_Wq"])),
        "wk": np.ascontiguousarray(np.asarray(inputs["L1_attn_Wk"])),
        "wv": np.ascontiguousarray(np.asarray(inputs["L1_attn_Wv"])),
        "wb": np.ascontiguousarray(np.asarray(inputs["L1_attn_Wbeta"])),
        "wo": np.ascontiguousarray(np.asarray(inputs["L1_attn_Wo"])),
        "gnorm": np.ascontiguousarray(np.asarray(inputs["L1_attn_g_norm"])),
    }


def build_l1a_program():
    nc = bass.Bass("TRN2", target_bir_lowering=False)
    k = Ker(nc)
    s2 = nc.dram_tensor("s2_tm", [BL, T, D], F32, kind="ExternalInput").ap()
    wq = nc.dram_tensor("wq", [D, D], F32, kind="ExternalInput").ap()
    wk = nc.dram_tensor("wk", [D, D], F32, kind="ExternalInput").ap()
    wv = nc.dram_tensor("wv", [D, D], F32, kind="ExternalInput").ap()
    wb = nc.dram_tensor("wb", [D, 4], F32, kind="ExternalInput").ap()
    wo = nc.dram_tensor("wo", [D, D], F32, kind="ExternalInput").ap()
    gn = nc.dram_tensor("gnorm", [DK], F32, kind="ExternalInput").ap()
    s3 = nc.dram_tensor("s3_tm", [BL, T, D], F32, kind="ExternalOutput").ap()
    emit_l1a(k, s2, wq, wk, wv, wb, wo, gn, s3)
    k.ctx.close()
    return nc




DFF2 = 4096
N1G = 16 * 2048 * 1024
N2G = 16 * 2048 * 4096
T1 = 167772
T2 = 671088
RBIG = 12582912.0


def emit_l1f(k: Ker, s3_tm, ln_g, ln_b, w1T, w2T, s4_tm, hln, gsp, ccb):
    nc = k.nc
    iot = k.sb("lf_iot", [128, 128], F32)
    identb = k.sb("lf_idb", [128, 128], BF16)
    identf = k.sb("lf_idf", [128, 128], F32)
    ones1f = k.sb("lf_o1f", [1, 128], F32)
    gain = k.sb("lf_gain", [128, D], F32)
    bias = k.sb("lf_bias", [128, D], F32)
    grow = k.sb("lf_gr", [1, D], F32)
    brow = k.sb("lf_br", [1, D], F32)
    wld = k.sb("lf_wld", [128, DFF2], F32)
    wld2 = k.sb("lf_wld2", [128, DFF2], F32)
    g32 = k.sb("lf_g32", [128, DFF2], F32)
    t32 = k.sb("lf_t32", [128, DFF2], F32)
    jbf = k.sb("lf_jbf", [128, DFF2], BF16)
    qtm = k.sb("lf_qtm", [128, DFF2], BF16)
    qst = k.sb("lf_qst", [128, 16, 34], F32)
    srow = k.sb("lf_srow", [1, 16], F32)
    prow = k.sb("lf_prow", [1, 16], F32)
    m8r = k.sb("lf_m8r", [1, 16], U8)
    invg = k.sb("lf_invg", [128, 1], F32)
    mcol = k.sb("lf_mcol", [128, 4], F32)
    resid = k.sb("lf_res", [128, D], F32)
    stile = k.sb("lf_sti", [128, D], F32)
    junk = k.sb("lf_jk", [128, D], F32)
    ocol = k.sb("lf_oc", [128, 8], F32)
    thp = k.sb("lf_thp", [128, 8], F32)
    lf_ctx = ExitStack()
    w1q = lf_ctx.enter_context(nc.sbuf_tensor("lf_w1q", [128, NCH, DFF2], BF16))
    xqc = lf_ctx.enter_context(nc.sbuf_tensor("lf_xqc", [128, NCH, 128], BF16))
    psA = k.ps("lf_psA", [128, 4, 512], F32)
    psB = k.ps("lf_psB", [128, 4, 512], F32)
    psAb = psA.ap().bitcast(BF16)
    k.lf = dict(thp=thp, identf=identf, ones1f=ones1f, psA=psA, invg=invg, jbf=jbf,
                qst=qst, srow=srow, prow=prow, m8r=m8r, wld=wld, wld2=wld2,
                t32=t32, g32=g32, mcol=mcol, ccb=ccb)

    ld = k.sem("ld"); st = k.sem("st"); tc = k.sem("tc")
    vc = k.sem("vc"); sc = k.sem("sc")
    k.ccn = getattr(k, "ccn", 0)

    l0 = ld.n
    with nc.Block(no_gpsimd_drain=True) as b:
        @b.gpsimd
        def _(g):
            g.iota(iot[:, :], [[1, 128]], base=0, channel_multiplier=-1,
                   allow_small_or_imprecise_dtypes=True)
        @b.sync
        def _(sy):
            fdma(ld, sy.dma_start(out=grow[:, :], in_=ln_g.unsqueeze(0)))
            fdma(ld, sy.dma_start(out=brow[:, :], in_=ln_b.unsqueeze(0)))
    with nc.Block(no_gpsimd_drain=True) as b:
        @b.vector
        def _(v):
            v.memset(ocol[:, 0:1], 1e-5)
            v.memset(ones1f[:, :], 1.0)
            v.tensor_scalar(identb[:, :], iot[:, :], 0.0, None, ALU.is_equal)
            v.tensor_scalar(identf[:, :], iot[:, :], 0.0, None, ALU.is_equal)
            v.wait_ge(ld.h, l0 + 32)
    t0 = tc.n
    with nc.Block(no_gpsimd_drain=True) as b:
        @b.tensor
        def _(t):
            for q in range(2):
                fin(tc, t.matmul(psA[:, q, :], ones1f[:, :],
                                 grow[:, q * 512:(q + 1) * 512], start=True, stop=True))
                fin(tc, t.matmul(psA[:, 2 + q, :], ones1f[:, :],
                                 brow[:, q * 512:(q + 1) * 512], start=True, stop=True))
        @b.vector
        def _(v):
            v.wait_ge(tc.h, t0 + 4)
            v.tensor_copy(gain[:, 0:512], psA[:, 0, :])
            v.tensor_copy(gain[:, 512:1024], psA[:, 1, :])
            v.tensor_copy(bias[:, 0:512], psA[:, 2, :])
            v.tensor_copy(bias[:, 512:1024], psA[:, 3, :])

    for ti in range(32):
        bi, tci = ti // 16, ti % 16
        li = ld.n
        with nc.Block(no_gpsimd_drain=True) as b:
            @b.sync
            def _(sy):
                fdma(ld, sy.dma_start(out=resid[:, :],
                                      in_=s3_tm[bi, tci * 128:(tci + 1) * 128, :]))
            @b.vector
            def _(v):
                v.wait_ge(ld.h, li + 16)
                v.memset(junk[:, :], 0.0)
        _ln_chunk(k, junk[:, :], resid, stile, junk, ocol, vc, sc,
                  gain=gain, bias=bias)
        v0 = vc.n; s0 = st.n
        with nc.Block(no_gpsimd_drain=True) as b:
            @b.vector
            def _(v):
                fin(vc, v.tensor_scalar(jbf[:, 0:D], stile[:, :], 0.0, -1e30,
                                        ALU.add, ALU.max,
                                        accum_out=qst[:, 0, ti:ti + 1]))
                fin(vc, v.tensor_scalar(jbf[:, 0:D], stile[:, :], 0.0, 1e30,
                                        ALU.add, ALU.min,
                                        accum_out=qst[:, 7, ti:ti + 1]))
            @b.scalar
            def _(s):
                fin(sc, s.activation(jbf.ap().bitcast(F32)[:, D:D + D], stile[:, :],
                                     AF.Square, accum_out=qst[:, 1, ti:ti + 1]))
            @b.sync
            def _(sy):
                fdma(st, sy.dma_start(out=hln[bi, tci * 128:(tci + 1) * 128, :],
                                      in_=stile[:, :]))
                sy.wait_ge(st.h, s0 + 16)

    _tern_gamma(k, w1T, NCH, DFF2, 2, ld, vc, sc, tc)
    _quant_solve(k, hln, N1G, T1, two_sided=True, anchor_gelu=False,
                 nt=8, FW=4096, stat_max=0, stat_sq=1,
                 ld=ld, tc=tc, vc=vc, sc=sc)
    _scale_from_amax(k, gslot=2, scol=0, sclcol=1, tc=tc, vc=vc)
    _tern_quant(k, w1T, NCH, DFF2, w1q, 2, ld, vc, tc)

    for ti in range(32):
        bi, tci = ti // 16, ti % 16
        t0 = tc.n; v0 = vc.n; c0 = sc.n; s0 = st.n; li = ld.n
        with nc.Block(no_gpsimd_drain=True) as b:
            @b.sync
            def _(sy):
                fdma(ld, sy.dma_start(out=resid[:, :],
                                      in_=hln[bi, tci * 128:(tci + 1) * 128, :]))
            @b.tensor
            def _(t):
                t.wait_ge(vc.h, v0 + 3)
                for ci in range(NCH):
                    fin(tc, t.transpose(psAb[:, 3, ci * 128:(ci + 1) * 128],
                                        jbf[:, ci * 128:(ci + 1) * 128],
                                        identb[:, :]))
                t.wait_ge(vc.h, v0 + 4)
                for pc in range(8):
                    ps = psA if pc < 4 else psB
                    for ci in range(NCH):
                        mm = t.matmul(ps[:, pc % 4, :],
                                      xqc[:, ci, :],
                                      w1q[:, ci, pc * 512:(pc + 1) * 512],
                                      start=(ci == 0), stop=(ci == NCH - 1))
                    fin(tc, mm)
            @b.vector
            def _(v):
                v.wait_ge(ld.h, li + 16)
                t1 = fin(vc, v.tensor_scalar(stile[:, :], resid[:, :], mcol[:, 0:1],
                                             RBIG, ALU.mult, ALU.add))
                v.wait_ge(vc.h, t1)
                t2 = fin(vc, v.tensor_scalar(stile[:, :], stile[:, :], RBIG,
                                             127.0, ALU.subtract, ALU.min))
                v.wait_ge(vc.h, t2)
                t3 = fin(vc, v.tensor_scalar(jbf[:, 0:D], stile[:, :], -128.0, None,
                                             ALU.max))
                v.wait_ge(vc.h, t3)
                v.wait_ge(tc.h, t0 + 8)
                t4 = fin(vc, v.tensor_copy(xqc.ap(),
                                           psAb[:, 3, :].rearrange("p (a b) -> p a b",
                                                                   a=8)))
                v.wait_ge(vc.h, t4)
                v.wait_ge(tc.h, t0 + 8 + 4)
                t5 = fin(vc, v.tensor_scalar(t32[:, 0:2048],
                                             psA.ap().rearrange("p a b -> p (a b)"),
                                             mcol[:, 1:2], None, ALU.mult))
                v.wait_ge(tc.h, t0 + 8 + 8)
                t6 = fin(vc, v.tensor_scalar(t32[:, 2048:4096],
                                             psB.ap().rearrange("p a b -> p (a b)"),
                                             mcol[:, 1:2], None, ALU.mult))
                v.wait_ge(sc.h, c0 + 1)
                fin(vc, v.tensor_scalar(jbf[:, :], g32[:, :], 0.0, -1e30,
                                        ALU.add, ALU.max,
                                        accum_out=qst[:, 2, ti:ti + 1]))
            @b.scalar
            def _(s):
                s.wait_ge(vc.h, v0 + 6)
                t1 = fin(sc, s.activation(g32[:, :], t32[:, :], AF.Gelu))
                s.wait_ge(sc.h, t1)
                fin(sc, s.activation(jbf[:, :], t32[:, :], AF.Square,
                                     accum_out=qst[:, 3, ti:ti + 1]))
                fin(sc, s.activation(jbf[:, :], g32[:, :], AF.Square,
                                     accum_out=qst[:, 4, ti:ti + 1]))
            @b.sync
            def _(sy):
                sy.wait_ge(sc.h, c0 + 1)
                fdma(st, sy.dma_start(out=gsp[bi, tci * 128:(tci + 1) * 128, :],
                                      in_=g32[:, :]))
                sy.wait_ge(st.h, s0 + 16)

    lf_ctx.close()
    w2q = k.sb("lf_w2q", [128, 32, D], BF16)
    xq2f = k.sb("lf_xq2f", [128, 32, 128], BF16)
    _tern_gamma(k, w2T, 32, D, 7, ld, vc, sc, tc)
    _quant_solve(k, gsp, N2G, T2, two_sided=False, anchor_gelu=True,
                 nt=32, FW=4096, stat_max=2, stat_sq=3,
                 ld=ld, tc=tc, vc=vc, sc=sc)
    _scale_from_amax(k, gslot=7, scol=2, sclcol=3, tc=tc, vc=vc)
    _tern_quant(k, w2T, 32, D, w2q, 7, ld, vc, tc)

    for ti in range(32):
        bi, tci = ti // 16, ti % 16
        li = ld.n; v0 = vc.n; t0 = tc.n; c0 = sc.n
        with nc.Block(no_gpsimd_drain=True) as b:
            @b.sync
            def _(sy):
                fdma(ld, sy.dma_start(out=g32[:, :],
                                      in_=gsp[bi, tci * 128:(tci + 1) * 128, :]))
                fdma(ld, sy.dma_start(out=resid[:, :],
                                      in_=s3_tm[bi, tci * 128:(tci + 1) * 128, :]))
            @b.vector
            def _(v):
                v.wait_ge(ld.h, li + 32)
                t1 = fin(vc, v.tensor_scalar(t32[:, :], g32[:, :], mcol[:, 2:3],
                                             RBIG, ALU.mult, ALU.add))
                v.wait_ge(vc.h, t1)
                t2 = fin(vc, v.tensor_scalar(t32[:, :], t32[:, :], RBIG,
                                             127.0, ALU.subtract, ALU.min))
                v.wait_ge(vc.h, t2)
                t3 = fin(vc, v.tensor_scalar(qtm[:, :], t32[:, :], -128.0, None,
                                             ALU.max))
                v.wait_ge(vc.h, t3)
                for qq in range(4):
                    v.wait_ge(tc.h, t0 + 8 * qq + 8)
                    fin(vc, v.tensor_copy(xq2f[:, 8 * qq:8 * qq + 8, :],
                                          psAb[:, qq, :]
                                          .rearrange("p (a b) -> p a b", a=8)))
                v.wait_ge(tc.h, t0 + 34)
                t9 = fin(vc, v.tensor_scalar(stile[:, :],
                                             psB[:, 0:2, :].rearrange("p a b -> p (a b)"),
                                             mcol[:, 3:4], None, ALU.mult))
                v.wait_ge(vc.h, t9)
            @b.tensor
            def _(t):
                for qq in range(4):
                    if qq == 0:
                        t.wait_ge(vc.h, v0 + 3)
                    for ci in range(8):
                        fin(tc, t.transpose(psAb[:, qq, ci * 128:(ci + 1) * 128],
                                            qtm[:, qq * 1024 + ci * 128:
                                                qq * 1024 + (ci + 1) * 128],
                                            identb[:, :]))
                t.wait_ge(vc.h, v0 + 7)
                for half in range(2):
                    for ci in range(32):
                        mm = t.matmul(psB[:, half, :],
                                      xq2f[:, ci, :],
                                      w2q[:, ci, half * 512:(half + 1) * 512],
                                      start=(ci == 0), stop=(ci == 31))
                    fin(tc, mm)

        _ln_chunk(k, stile[:, :], resid, stile, junk, ocol, vc, sc,
                  store=(s4_tm[bi, tci * 128:(tci + 1) * 128, :], st))


def _allreduce(k, op, lo_slot, n_slots, ld):
    nc = k.nc
    prow = k.lf["prow"]
    cin, cout = k.lf["ccb"][k.ccn % len(k.lf["ccb"])]
    ccs = k.sem(f"cc{k.ccn}"); k.ccn += 1
    l0 = ld.n
    with nc.Block(no_gpsimd_drain=True) as b:
        @b.sync
        def _(sy):
            fdma(ld, sy.dma_start(out=cin, in_=prow[:, 0:16]))
        @b.gpsimd
        def _(g):
            g.wait_ge(ld.h, l0 + 16)
            g.collective_compute("AllReduce", op, replica_groups=[list(range(8))],
                                 ins=[cin.opt()], outs=[cout.opt()]).then_inc(ccs.h)
            g.wait_ge(ccs.h, 1)
            fdma(ld, g.dma_start(out=prow[:, lo_slot:lo_slot + n_slots],
                                 in_=cout[:, lo_slot:lo_slot + n_slots]))
            g.wait_ge(ld.h, l0 + 32)


def _colreduce(k, cols_ap, n, out_row_slice, ops, tc, vc):
    nc = k.nc
    identf = k.lf["identf"]; psA = k.lf["psA"]
    for j in range(n):
        t0 = tc.n
        with nc.Block(no_gpsimd_drain=True) as b:
            @b.tensor
            def _(t):
                fin(tc, t.transpose(psA[0:1, 3, 0:128], cols_ap[:, j:j + 1],
                                    identf[:, :]))
            @b.vector
            def _(v):
                v.wait_ge(tc.h, t0 + 1)
                t1 = fin(vc, v.tensor_reduce(out_row_slice[:, j:j + 1],
                                             psA[0:1, 3, 0:128], axis=AX.X,
                                             op=ops[j]))
                v.wait_ge(vc.h, t1)


def _bcast_col(k, src11, dstcol, tc, vc):
    nc = k.nc
    ones1f = k.lf["ones1f"]; psA = k.lf["psA"]
    t0 = tc.n
    with nc.Block(no_gpsimd_drain=True) as b:
        @b.tensor
        def _(t):
            fin(tc, t.matmul(psA[:, 3, 0:1], ones1f[:, :], src11, start=True,
                             stop=True))
        @b.vector
        def _(v):
            v.wait_ge(tc.h, t0 + 1)
            t1 = fin(vc, v.tensor_copy(dstcol, psA[:, 3, 0:1]))
            v.wait_ge(vc.h, t1)


def _tern_gamma(k, wT, nci, W, gslot, ld, vc, sc, tc):
    nc = k.nc
    qst = k.lf["qst"]; srow = k.lf["srow"]; jbf = k.lf["jbf"]
    wld = k.lf["wld"]; wld2 = k.lf["wld2"]
    ldp = [k.sem("ldp0"), k.sem("ldp1")]
    P0_ = [ldp[0].n, ldp[1].n]; C0_ = sc.n
    with nc.Block(no_gpsimd_drain=True) as b:
        @b.sync
        def _(sy):
            for ci in range(nci):
                if ci >= 2:
                    sy.wait_ge(sc.h, C0_ + ci - 1)
                fdma(ldp[ci % 2], sy.dma_start(
                    out=(wld if ci % 2 == 0 else wld2)[:, 0:W],
                    in_=wT[ci * 128:(ci + 1) * 128, :]))
        @b.scalar
        def _(s):
            for ci in range(nci):
                buf = wld if ci % 2 == 0 else wld2
                s.wait_ge(ldp[ci % 2].h, P0_[ci % 2] + 16 * (ci // 2 + 1))
                fin(sc, s.activation(jbf[:, 0:W], buf[:, 0:W], AF.Abs,
                                     accum_out=qst[:, 5, ci:ci + 1]))
    v0 = vc.n
    with nc.Block(no_gpsimd_drain=True) as b:
        @b.vector
        def _(v):
            t1 = fin(vc, v.tensor_reduce(qst[:, 6, 2:3], qst[:, 5, 0:nci],
                                         axis=AX.X, op=ALU.add))
            v.wait_ge(vc.h, t1)
    _colreduce(k, qst[:, 6, 2:3], 1, k.lf["srow"][:, 8:9], [ALU.add], tc, vc)
    v0 = vc.n
    with nc.Block(no_gpsimd_drain=True) as b:
        @b.vector
        def _(v):
            t1 = fin(vc, v.tensor_scalar(srow[:, gslot:gslot + 1], srow[:, 8:9],
                                         1.0 / (nci * 128 * W), 1e-5,
                                         ALU.mult, ALU.add))
            v.wait_ge(vc.h, t1)


def _tern_quant(k, wT, nci, W, wq, gslot, ld, vc, tc):
    nc = k.nc
    srow = k.lf["srow"]; invg = k.lf["invg"]
    wld = k.lf["wld"]; wld2 = k.lf["wld2"]
    v0 = vc.n
    with nc.Block(no_gpsimd_drain=True) as b:
        @b.vector
        def _(v):
            t1 = fin(vc, v.reciprocal(srow[:, 9:10], srow[:, gslot:gslot + 1]))
            v.wait_ge(vc.h, t1)
    _bcast_col(k, srow[:, 9:10], invg[:, 0:1], tc, vc)
    ldp = [k.sem("ldp0"), k.sem("ldp1")]
    P0_ = [ldp[0].n, ldp[1].n]; V0_ = vc.n
    with nc.Block(no_gpsimd_drain=True) as b:
        @b.sync
        def _(sy):
            for ci in range(nci):
                if ci >= 2:
                    sy.wait_ge(vc.h, V0_ + 3 * (ci - 1))
                fdma(ldp[ci % 2], sy.dma_start(
                    out=(wld if ci % 2 == 0 else wld2)[:, 0:W],
                    in_=wT[ci * 128:(ci + 1) * 128, :]))
        @b.vector
        def _(v):
            for ci in range(nci):
                buf = wld if ci % 2 == 0 else wld2
                v.wait_ge(ldp[ci % 2].h, P0_[ci % 2] + 16 * (ci // 2 + 1))
                t1 = fin(vc, v.tensor_scalar(buf[:, 0:W], buf[:, 0:W], invg[:, 0:1],
                                             RBIG, ALU.mult, ALU.add))
                v.wait_ge(vc.h, t1)
                t2 = fin(vc, v.tensor_scalar(buf[:, 0:W], buf[:, 0:W], RBIG, 1.0,
                                             ALU.subtract, ALU.min))
                v.wait_ge(vc.h, t2)
                t3 = fin(vc, v.tensor_scalar(wq[:, ci, :], buf[:, 0:W], -1.0, None,
                                             ALU.max))
                v.wait_ge(vc.h, t3)


def _scale_from_amax(k, gslot, scol, sclcol, tc, vc):
    nc = k.nc
    srow = k.lf["srow"]; mcol = k.lf["mcol"]
    v0 = vc.n
    with nc.Block(no_gpsimd_drain=True) as b:
        @b.vector
        def _(v):
            t1 = fin(vc, v.reciprocal(srow[:, 10:11], srow[:, 0:1]))
            v.wait_ge(vc.h, t1)
            t2 = fin(vc, v.tensor_scalar(srow[:, 11:12], srow[:, 10:11], 127.0,
                                         None, ALU.mult))
            t3 = fin(vc, v.tensor_tensor(srow[:, 12:13], srow[:, gslot:gslot + 1],
                                         srow[:, 0:1], ALU.mult))
            v.wait_ge(vc.h, t3)
            t4 = fin(vc, v.tensor_scalar(srow[:, 12:13], srow[:, 12:13],
                                         1.0 / 127.0, None, ALU.mult))
            v.wait_ge(vc.h, t4)
    _bcast_col(k, srow[:, 11:12], mcol[:, scol:scol + 1], tc, vc)
    _bcast_col(k, srow[:, 12:13], mcol[:, sclcol:sclcol + 1], tc, vc)


def _count_pass(k, dview, nt, FW, thp, thbase, m, two_sided, out_base, ld, tc, vc):
    nc = k.nc
    qst = k.lf["qst"]; jbf = k.lf["jbf"]
    wld = k.lf["wld"]; wld2 = k.lf["wld2"]; prow = k.lf["prow"]
    fpt = m * (2 if two_sided else 1)
    ldp = [k.sem("ldp0"), k.sem("ldp1")]
    P0_ = [ldp[0].n, ldp[1].n]; V0_ = vc.n
    with nc.Block(no_gpsimd_drain=True) as b:
        @b.sync
        def _(sy):
            for ti in range(nt):
                if ti >= 2:
                    sy.wait_ge(vc.h, V0_ + fpt * (ti - 1))
                fdma(ldp[ti % 2], sy.dma_start(
                    out=(wld if ti % 2 == 0 else wld2)[:, 0:FW], in_=dview[ti]))
        @b.vector
        def _(v):
            for ti in range(nt):
                buf = wld if ti % 2 == 0 else wld2
                v.wait_ge(ldp[ti % 2].h, P0_[ti % 2] + 16 * (ti // 2 + 1))
                for j in range(m):
                    fin(vc, v.tensor_scalar(jbf[:, 0:FW], buf[:, 0:FW],
                                            thp[:, thbase + j:thbase + j + 1], 0.0,
                                            ALU.is_gt, ALU.add,
                                            accum_out=qst[:, 8 + j, ti:ti + 1]))
                    if two_sided:
                        fin(vc, v.tensor_scalar(jbf[:, 0:FW], buf[:, 0:FW],
                                                thp[:, thbase + m + j:thbase + m + j + 1],
                                                0.0, ALU.is_lt, ALU.add,
                                                accum_out=qst[:, 11 + j, ti:ti + 1]))
            v.wait_ge(vc.h, vc.n)
    v0 = vc.n
    with nc.Block(no_gpsimd_drain=True) as b:
        @b.vector
        def _(v):
            for j in range(m):
                fin(vc, v.tensor_reduce(qst[:, 6, 8 + j:9 + j], qst[:, 8 + j, 0:nt],
                                        axis=AX.X, op=ALU.add))
                if two_sided:
                    fin(vc, v.tensor_reduce(qst[:, 6, 11 + j:12 + j],
                                            qst[:, 11 + j, 0:nt],
                                            axis=AX.X, op=ALU.add))
            v.wait_ge(vc.h, vc.n)
    if two_sided:
        v0 = vc.n
        with nc.Block(no_gpsimd_drain=True) as b:
            @b.vector
            def _(v):
                t1 = fin(vc, v.tensor_tensor(qst[:, 6, 8:8 + m], qst[:, 6, 8:8 + m],
                                             qst[:, 6, 11:11 + m], ALU.add))
                v.wait_ge(vc.h, t1)
    _colreduce(k, qst[:, 6, 8:8 + m], m, prow[:, out_base:out_base + m],
               [ALU.add] * m, tc, vc)
    _allreduce(k, ALU.add, out_base, m, ld)


def _quant_solve(k, data_dram, NG, TRANK, two_sided, anchor_gelu, nt, FW,
                 stat_max, stat_sq, ld, tc, vc, sc):
    nc = k.nc
    qst = k.lf["qst"]; srow = k.lf["srow"]; prow = k.lf["prow"]
    m8r = k.lf["m8r"]; jbf = k.lf["jbf"]
    wld = k.lf["wld"]; wld2 = k.lf["wld2"]; t32 = k.lf["t32"]; g32 = k.lf["g32"]
    dview = data_dram.rearrange("b t d -> (b t d)").rearrange("(n p f) -> n p f",
                                                              p=128, f=FW)
    thp = k.lf["thp"]

    v0 = vc.n
    with nc.Block(no_gpsimd_drain=True) as b:
        @b.vector
        def _(v):
            t1 = fin(vc, v.tensor_reduce(qst[:, 6, 0:1], qst[:, stat_max, 0:32],
                                         axis=AX.X, op=ALU.max))
            t2 = fin(vc, v.tensor_reduce(qst[:, 6, 1:2], qst[:, stat_sq, 0:32],
                                         axis=AX.X, op=ALU.add))
            if two_sided:
                t3 = fin(vc, v.tensor_reduce(qst[:, 6, 3:4], qst[:, 7, 0:32],
                                             axis=AX.X, op=ALU.min))
                v.wait_ge(vc.h, t3)
                t4 = fin(vc, v.tensor_scalar(qst[:, 6, 3:4], qst[:, 6, 3:4], -1.0,
                                             None, ALU.mult))
                v.wait_ge(vc.h, t4)
                t5 = fin(vc, v.tensor_tensor(qst[:, 6, 0:1], qst[:, 6, 0:1],
                                             qst[:, 6, 3:4], ALU.max))
            v.wait_ge(vc.h, vc.n)
    _colreduce(k, qst[:, 6, 0:2], 2, prow[:, 0:2], [ALU.max, ALU.add], tc, vc)
    _allreduce(k, ALU.max, 0, 1, ld)
    _allreduce(k, ALU.add, 1, 1, ld)
    v0 = vc.n; c0 = sc.n
    with nc.Block(no_gpsimd_drain=True) as b:
        @b.vector
        def _(v):
            t1 = fin(vc, v.tensor_scalar(prow[:, 2:3], prow[:, 1:2], 1.0 / NG,
                                         None, ALU.mult))
            v.wait_ge(vc.h, t1)
        @b.scalar
        def _(s):
            s.wait_ge(vc.h, v0 + 1)
            t1 = fin(sc, s.activation(prow[:, 3:4], prow[:, 2:3], AF.Sqrt))
            s.wait_ge(sc.h, t1)
            if anchor_gelu:
                t2 = fin(sc, s.activation(prow[:, 3:4], prow[:, 3:4], AF.Gelu,
                                          scale=2.5758))
            else:
                t2 = fin(sc, s.activation(prow[:, 3:4], prow[:, 3:4], AF.Copy,
                                          scale=2.8070))
            s.wait_ge(sc.h, t2)

    v0 = vc.n
    with nc.Block(no_gpsimd_drain=True) as b:
        @b.vector
        def _(v):
            v.memset(srow[:, 13:14], float(TRANK))
            for j, f in enumerate([0.78, 1.0, 1.26]):
                fin(vc, v.tensor_scalar(srow[:, 3 + j:4 + j], prow[:, 3:4], f,
                                        None, ALU.mult))
            v.wait_ge(vc.h, vc.n)
    for j in range(3):
        _bcast_col(k, srow[:, 3 + j:4 + j], thp[:, j:j + 1], tc, vc)
        if two_sided:
            v0 = vc.n
            with nc.Block(no_gpsimd_drain=True) as b:
                @b.vector
                def _(v):
                    t1 = fin(vc, v.tensor_scalar(srow[:, 14:15], srow[:, 3 + j:4 + j],
                                                 -1.0, None, ALU.mult))
                    v.wait_ge(vc.h, t1)
            _bcast_col(k, srow[:, 14:15], thp[:, 3 + j:4 + j], tc, vc)
    _count_pass(k, dview, nt, FW, thp, 0, 3, two_sided, 8, ld, tc, vc)
    v0 = vc.n
    with nc.Block(no_gpsimd_drain=True) as b:
        @b.vector
        def _(v):
            t1 = fin(vc, v.tensor_scalar(prow[:, 11:14], prow[:, 8:11],
                                         srow[:, 13:14], None, ALU.is_ge))
            v.wait_ge(vc.h, t1)
            t2 = fin(vc, v.tensor_tensor(srow[:, 8:11], srow[:, 3:6],
                                         prow[:, 11:14], ALU.mult))
            v.wait_ge(vc.h, t2)
            t3 = fin(vc, v.tensor_scalar(srow[:, 8:11], srow[:, 8:11], 0.0, 0.0,
                                         ALU.add, ALU.max, accum_out=prow[:, 4:5]))
            v.wait_ge(vc.h, t3)
            t4 = fin(vc, v.tensor_tensor(srow[:, 8:11],
                                         prow[:, 0:1].broadcast_to([1, 3]),
                                         srow[:, 3:6], ALU.subtract))
            v.wait_ge(vc.h, t4)
            t5 = fin(vc, v.tensor_tensor(srow[:, 8:11], srow[:, 8:11],
                                         prow[:, 11:14], ALU.mult))
            v.wait_ge(vc.h, t5)
            t6 = fin(vc, v.tensor_tensor(srow[:, 8:11], srow[:, 8:11],
                                         srow[:, 3:6], ALU.add))
            v.wait_ge(vc.h, t6)
            t7 = fin(vc, v.tensor_scalar(srow[:, 8:11], srow[:, 8:11], 0.0, 1e30,
                                         ALU.add, ALU.min, accum_out=prow[:, 5:6]))
            v.wait_ge(vc.h, t7)
            t8 = fin(vc, v.tensor_tensor(prow[:, 5:6], prow[:, 5:6], prow[:, 0:1],
                                         ALU.min))
            v.wait_ge(vc.h, t8)
            t9 = fin(vc, v.tensor_tensor(prow[:, 3:4], prow[:, 4:5], prow[:, 5:6],
                                         ALU.add))
            v.wait_ge(vc.h, t9)
            t10 = fin(vc, v.tensor_scalar(prow[:, 3:4], prow[:, 3:4], 0.5, None,
                                          ALU.mult))
            v.wait_ge(vc.h, t10)
        @b.scalar
        def _(s):
            s.wait_ge(vc.h, v0 + 1)
            t1 = fin(sc, s.activation(srow[:, 15:16], srow[:, 13:14], AF.Ln))
            s.wait_ge(sc.h, t1)

    for it in range(3):
        _bcast_col(k, prow[:, 3:4], thp[:, 0:1], tc, vc)
        if two_sided:
            v0 = vc.n
            with nc.Block(no_gpsimd_drain=True) as b:
                @b.vector
                def _(v):
                    t1 = fin(vc, v.tensor_scalar(srow[:, 14:15], prow[:, 3:4],
                                                 -1.0, None, ALU.mult))
                    v.wait_ge(vc.h, t1)
            _bcast_col(k, srow[:, 14:15], thp[:, 1:2], tc, vc)
        _count_pass(k, dview, nt, FW, thp, 0, 1, two_sided, 6, ld, tc, vc)
        v0 = vc.n; c0 = sc.n
        with nc.Block(no_gpsimd_drain=True) as b:
            @b.vector
            def _(v):
                t1 = fin(vc, v.tensor_scalar(prow[:, 7:8], prow[:, 6:7], 0.5, None,
                                             ALU.max))
                v.wait_ge(vc.h, t1)
            @b.scalar
            def _(s):
                s.wait_ge(vc.h, v0 + 1)
                t1 = fin(sc, s.activation(prow[:, 7:8], prow[:, 7:8], AF.Ln))
                s.wait_ge(sc.h, t1)
        v0 = vc.n
        with nc.Block(no_gpsimd_drain=True) as b:
            @b.vector
            def _(v):
                t1 = fin(vc, v.tensor_scalar(m8r[:, 0:1], prow[:, 6:7],
                                             srow[:, 13:14], None, ALU.is_ge))
                v.wait_ge(vc.h, t1)
                t2 = fin(vc, v.select(prow[:, 4:5], m8r[:, 0:1], prow[:, 3:4],
                                      prow[:, 4:5]))
                v.wait_ge(vc.h, t2)
                t3 = fin(vc, v.select(prow[:, 5:6], m8r[:, 0:1], prow[:, 5:6],
                                      prow[:, 3:4]))
                v.wait_ge(vc.h, t3)
                t4 = fin(vc, v.tensor_tensor(srow[:, 14:15], prow[:, 7:8],
                                             srow[:, 15:16], ALU.subtract))
                t5 = fin(vc, v.reciprocal(srow[:, 12:13], prow[:, 3:4]))
                v.wait_ge(vc.h, t5)
                t6 = fin(vc, v.tensor_tensor(srow[:, 14:15], srow[:, 14:15],
                                             prow[:, 2:3], ALU.mult))
                v.wait_ge(vc.h, t6)
                t7 = fin(vc, v.tensor_tensor(srow[:, 14:15], srow[:, 14:15],
                                             srow[:, 12:13], ALU.mult))
                v.wait_ge(vc.h, t7)
                t8 = fin(vc, v.tensor_tensor(srow[:, 14:15], srow[:, 14:15],
                                             prow[:, 3:4], ALU.add))
                v.wait_ge(vc.h, t8)
                t9 = fin(vc, v.tensor_tensor(srow[:, 12:13], prow[:, 5:6],
                                             prow[:, 4:5], ALU.subtract))
                v.wait_ge(vc.h, t9)
                t10 = fin(vc, v.tensor_scalar(srow[:, 12:13], srow[:, 12:13], 0.03,
                                              None, ALU.mult))
                v.wait_ge(vc.h, t10)
                t11 = fin(vc, v.tensor_tensor(srow[:, 11:12], prow[:, 4:5],
                                              srow[:, 12:13], ALU.add))
                v.wait_ge(vc.h, t11)
                t12 = fin(vc, v.tensor_tensor(srow[:, 14:15], srow[:, 14:15],
                                              srow[:, 11:12], ALU.max))
                v.wait_ge(vc.h, t12)
                t13 = fin(vc, v.tensor_tensor(srow[:, 11:12], prow[:, 5:6],
                                              srow[:, 12:13], ALU.subtract))
                v.wait_ge(vc.h, t13)
                t14 = fin(vc, v.tensor_tensor(prow[:, 3:4], srow[:, 14:15],
                                              srow[:, 11:12], ALU.min))
                v.wait_ge(vc.h, t14)

    _bcast_col(k, prow[:, 5:6], thp[:, 0:1], tc, vc)
    if two_sided:
        v0 = vc.n
        with nc.Block(no_gpsimd_drain=True) as b:
            @b.vector
            def _(v):
                t1 = fin(vc, v.tensor_scalar(srow[:, 14:15], prow[:, 5:6], -1.0,
                                             None, ALU.mult))
                v.wait_ge(vc.h, t1)
        _bcast_col(k, srow[:, 14:15], thp[:, 1:2], tc, vc)
    fpt = 6 if two_sided else 3
    ldp = [k.sem("ldp0"), k.sem("ldp1")]
    P0_ = [ldp[0].n, ldp[1].n]; V0_ = vc.n
    with nc.Block(no_gpsimd_drain=True) as b:
        @b.sync
        def _(sy):
            for ti in range(nt):
                if ti >= 2:
                    sy.wait_ge(vc.h, V0_ + fpt * (ti - 1))
                fdma(ldp[ti % 2], sy.dma_start(
                    out=(wld if ti % 2 == 0 else wld2)[:, 0:FW], in_=dview[ti]))
        @b.vector
        def _(v):
            for ti in range(nt):
                buf = wld if ti % 2 == 0 else wld2
                v.wait_ge(ldp[ti % 2].h, P0_[ti % 2] + 16 * (ti // 2 + 1))
                t1 = fin(vc, v.tensor_scalar(t32[:, 0:FW], buf[:, 0:FW],
                                             thp[:, 0:1], None, ALU.is_gt))
                v.wait_ge(vc.h, t1)
                t2 = fin(vc, v.scalar_tensor_tensor(g32[:, 0:FW], t32[:, 0:FW],
                                                    -1e30, buf[:, 0:FW],
                                                    ALU.mult, ALU.add))
                v.wait_ge(vc.h, t2)
                t3 = fin(vc, v.tensor_scalar(jbf[:, 0:FW], g32[:, 0:FW], 0.0,
                                             -1e30, ALU.add, ALU.max,
                                             accum_out=qst[:, 8, ti:ti + 1]))
                if two_sided:
                    t4 = fin(vc, v.tensor_scalar(t32[:, 0:FW], buf[:, 0:FW],
                                                 thp[:, 1:2], None, ALU.is_lt))
                    v.wait_ge(vc.h, t4)
                    t5 = fin(vc, v.scalar_tensor_tensor(g32[:, 0:FW], t32[:, 0:FW],
                                                        1e30, buf[:, 0:FW],
                                                        ALU.mult, ALU.add))
                    v.wait_ge(vc.h, t5)
                    fin(vc, v.tensor_scalar(jbf[:, 0:FW], g32[:, 0:FW], 0.0,
                                            1e30, ALU.add, ALU.min,
                                            accum_out=qst[:, 9, ti:ti + 1]))
            v.wait_ge(vc.h, vc.n)
    v0 = vc.n
    with nc.Block(no_gpsimd_drain=True) as b:
        @b.vector
        def _(v):
            t1 = fin(vc, v.tensor_reduce(qst[:, 6, 4:5], qst[:, 8, 0:nt],
                                         axis=AX.X, op=ALU.max))
            if two_sided:
                t2 = fin(vc, v.tensor_reduce(qst[:, 6, 5:6], qst[:, 9, 0:nt],
                                             axis=AX.X, op=ALU.min))
            v.wait_ge(vc.h, vc.n)
    if two_sided:
        _colreduce(k, qst[:, 6, 4:6], 2, prow[:, 14:16], [ALU.max, ALU.min], tc, vc)
    else:
        _colreduce(k, qst[:, 6, 4:5], 1, prow[:, 14:15], [ALU.max], tc, vc)
    v0 = vc.n
    with nc.Block(no_gpsimd_drain=True) as b:
        @b.vector
        def _(v):
            if two_sided:
                t1 = fin(vc, v.tensor_scalar(prow[:, 15:16], prow[:, 15:16], -1.0,
                                             None, ALU.mult))
                v.wait_ge(vc.h, t1)
                t2 = fin(vc, v.tensor_tensor(prow[:, 14:15], prow[:, 14:15],
                                             prow[:, 15:16], ALU.max))
                v.wait_ge(vc.h, t2)
    _allreduce(k, ALU.max, 14, 1, ld)
    v0 = vc.n
    with nc.Block(no_gpsimd_drain=True) as b:
        @b.vector
        def _(v):
            t1 = fin(vc, v.tensor_scalar(k.lf["srow"][:, 0:1], prow[:, 14:15],
                                         1e-5, None, ALU.max))
            v.wait_ge(vc.h, t1)


def l1f_host_inputs(core, inputs, s3):
    return {
        **({"s3_tm": np.ascontiguousarray(s3[core * BL:(core + 1) * BL])}
           if s3 is not None else {}),
        "ln_g": np.ascontiguousarray(np.asarray(inputs["L1_ffn_ln_g"])),
        "ln_b": np.ascontiguousarray(np.asarray(inputs["L1_ffn_ln_b"])),
        "w1T": np.ascontiguousarray(np.asarray(inputs["L1_ffn_w1"]).T),
        "w2T": np.ascontiguousarray(np.asarray(inputs["L1_ffn_w2"]).T),
    }


def build_l1f_program():
    nc = bass.Bass("TRN2", target_bir_lowering=False)
    k = Ker(nc)
    s3 = nc.dram_tensor("s3_tm", [BL, T, D], F32, kind="ExternalInput").ap()
    lng = nc.dram_tensor("ln_g", [D], F32, kind="ExternalInput").ap()
    lnb = nc.dram_tensor("ln_b", [D], F32, kind="ExternalInput").ap()
    w1T = nc.dram_tensor("w1T", [D, DFF2], F32, kind="ExternalInput").ap()
    w2T = nc.dram_tensor("w2T", [DFF2, D], F32, kind="ExternalInput").ap()
    s4 = nc.dram_tensor("s4_tm", [BL, T, D], F32, kind="ExternalOutput").ap()
    hln = nc.dram_tensor("hln", [BL, T, D], F32).ap()
    gsp = nc.dram_tensor("gsp", [BL, T, DFF2], F32).ap()
    ccb = [(nc.dram_tensor(f"ccin{i}", [1, 16], F32).ap(),
            nc.dram_tensor(f"ccout{i}", [1, 16], F32).ap()) for i in range(2)]
    emit_l1f(k, s3, lng, lnb, w1T, w2T, s4, hln, gsp, ccb)
    k.ctx.close()
    return nc


_PROGS = {}


def build_fused_program():
    nc = bass.Bass("TRN2", target_bir_lowering=False)
    k = Ker(nc)
    EI, EO = "ExternalInput", "ExternalOutput"
    x_fm = nc.dram_tensor("x_fm", [BL, D, T], F32, kind=EI).ap()
    x_tm = nc.dram_tensor("x_tm", [BL, T, D], F32, kind=EI).ap()
    wkT = nc.dram_tensor("wkT", [D, D], F32, kind=EI).ap()
    wvT = nc.dram_tensor("wvT", [D, D], F32, kind=EI).ap()
    wrT = nc.dram_tensor("wrT", [D, D], F32, kind=EI).ap()
    woT = nc.dram_tensor("woT", [D, D], F32, kind=EI).ap()
    prm = nc.dram_tensor("prm", [128, NCH, 5], F32, kind=EI).ap()
    rw1 = nc.dram_tensor("rw1", [D, 128], F32, kind=EI).ap()
    rw2 = nc.dram_tensor("rw2", [128, 16], F32, kind=EI).ap()
    w1 = nc.dram_tensor("w1", [E_EXP, D, DFF], F32, kind=EI).ap()
    w2 = nc.dram_tensor("w2", [E_EXP, D, DFF], F32, kind=EI).ap()
    w3 = nc.dram_tensor("w3", [E_EXP, DFF, D], F32, kind=EI).ap()
    wq = nc.dram_tensor("wq", [D, D], F32, kind=EI).ap()
    wk = nc.dram_tensor("wk", [D, D], F32, kind=EI).ap()
    wv = nc.dram_tensor("wv", [D, D], F32, kind=EI).ap()
    wb = nc.dram_tensor("wb", [D, 4], F32, kind=EI).ap()
    wo = nc.dram_tensor("wo", [D, D], F32, kind=EI).ap()
    gn = nc.dram_tensor("gnorm", [DK], F32, kind=EI).ap()
    lng = nc.dram_tensor("ln_g", [D], F32, kind=EI).ap()
    lnb = nc.dram_tensor("ln_b", [D], F32, kind=EI).ap()
    s4 = nc.dram_tensor("s4_tm", [BL, T, D], F32, kind=EO).ap()
    s1 = nc.dram_tensor("s1_tm", [BL, T, D], F32).ap()
    s2 = nc.dram_tensor("s2_tm", [BL, T, D], F32).ap()
    s3 = nc.dram_tensor("s3_tm", [BL, T, D], F32).ap()
    scr_r = nc.dram_tensor("scr_r", [BL, NCH, 128, T], BF16).ap()
    scr_ry = nc.dram_tensor("scr_ry", [BL, NCH, 128, T], BF16).ap()
    acc = nc.dram_tensor("acc", [BL, T, D], F32).ap()
    hln = nc.dram_tensor("hln", [BL, T, D], F32).ap()
    gsp = nc.dram_tensor("gsp", [BL, T, DFF2], F32).ap()
    ccb = [(nc.dram_tensor(f"ccin{i}", [1, 16], F32).ap(),
            nc.dram_tensor(f"ccout{i}", [1, 16], F32).ap()) for i in range(2)]

    k.push_scope()
    emit_l0(k, x_fm, x_tm, wkT, wvT, wrT, woT, prm, s1, scr_r, scr_ry)
    k.pop_scope()
    k.push_scope()
    emit_moe(k, s1, rw1, rw2, w1, w2, w3, s2, acc)
    k.pop_scope()
    k.push_scope()
    emit_l1a(k, s2, wq, wk, wv, wb, wo, gn, s3)
    k.pop_scope()
    k.push_scope()
    emit_l1f(k, s3,
             lng, lnb,
             nc.dram_tensor("w1T", [D, DFF2], F32, kind=EI).ap(),
             nc.dram_tensor("w2T", [DFF2, D], F32, kind=EI).ap(),
             s4, hln, gsp, ccb)
    k.pop_scope()
    k.ctx.close()
    return nc


def fused_host_inputs(core, inputs):
    m = {}
    m.update(l0_host_inputs(core, inputs))
    m.update(moe_host_inputs(core, inputs, None))
    m.update(l1a_host_inputs(core, inputs, None))
    m.update(l1f_host_inputs(core, inputs, None))
    m.pop("s1_tm", None); m.pop("s2_tm", None); m.pop("s3_tm", None)
    return m


def kernel(**inputs):
    from concourse.bass_utils import run_bass_kernel_spmd
    if "fused" not in _PROGS:
        _PROGS["fused"] = build_fused_program()
    maps = [fused_host_inputs(c, inputs) for c in range(8)]
    r = run_bass_kernel_spmd(_PROGS["fused"], maps, core_ids=list(range(8)))
    s4 = np.concatenate([np.asarray(r.results[c]["s4_tm"]) for c in range(8)], axis=0)
    return s4.astype(np.float32)
